# version 05
# baseline (speedup 1.0000x reference)
import contextlib
import numpy as np
import concourse.bass as bass
import concourse.mybir as mybir
from concourse.bass_utils import run_bass_kernel_spmd

F32 = mybir.dt.float32
AF = mybir.ActivationFunctionType
ALU = mybir.AluOpType
AX = mybir.AxisListType

NCORE = 8
D = 1024
T = 2048
NT = 16
SEQ = 2
CH = 128
DIN = 2048
NH = 32
HD = 64
DS = 128
DPROJ = 4384
NXBC = 18
DE0 = 2048
H1 = 4
KD = 256
VD = 512
DE1 = 2816

BLOCKS = ("l0a", "l0f", "l1a", "l1f")
MOE_SPARSE = True


class Ctr:
    def __init__(self, sem):
        self.sem = sem
        self.n = 0


class Banks:
    def __init__(self, tensors):
        self.t = tensors
        self.rel = [None] * len(tensors)
        self.i = 0

    def alloc(self, nb):
        ids = [(self.i + j) % len(self.t) for j in range(nb)]
        self.i = (self.i + nb) % len(self.t)
        waits = [self.rel[b] for b in ids if self.rel[b] is not None]
        return ids, waits

    def set_rel(self, ids, ctrval):
        for b in ids:
            self.rel[b] = ctrval


class K:
    def __init__(self, nc, es):
        self.nc = nc
        self.es = es
        self.ctrs = {}
        psum = [es.enter_context(nc.psum_tensor(f"pb{i}", [128, 512], F32))
                for i in range(8)]
        self.banks = Banks(psum)
        self.uid = 0

    def sb(self, name, shape):
        return self.es.enter_context(self.nc.sbuf_tensor(name, list(shape), F32))

    def ctr(self, name):
        if name not in self.ctrs:
            sem = self.es.enter_context(self.nc.semaphore(name))
            self.ctrs[name] = Ctr(sem)
        return self.ctrs[name]

    def fresh(self, prefix):
        self.uid += 1
        return f"{prefix}{self.uid}"


class WRing:

    def __init__(self, k: K, name, nslots, sb=None):
        self.k = k
        self.nslots = nslots
        alloc = sb if sb is not None else k.sb
        self.buf = alloc(name, [128, nslots, 512])
        self.slot_ctr = [k.ctr(f"{name}_s{s}") for s in range(nslots)]
        self.loads = []
        self.base = [c.n for c in self.slot_ctr]
        self.freed = {}

    def begin(self):
        self.loads = []
        self.base = [c.n for c in self.slot_ctr]
        self.freed = {}

    def add(self, src_ap, rows, cols):
        self.loads.append((src_ap, rows, cols))
        return len(self.loads) - 1

    def tile(self, i):
        _, rows, cols = self.loads[i]
        return self.buf[:, i % self.nslots, :][0:rows, 0:cols]

    def wait(self, te, i):
        s = i % self.nslots
        te.wait_ge(self.slot_ctr[s].sem, self.base[s] + 16 * (i // self.nslots + 1))

    def emit_loader(self, block):
        if not self.loads:
            return
        loads, ns = self.loads, self.nslots
        buf, freed, sc, base = self.buf, self.freed, self.slot_ctr, self.base
        fw = getattr(self, "first_wait", None)

        @block.sync
        def _(sync):
            if fw is not None:
                sync.wait_ge(fw[0].sem, fw[1])
            for i, (src, rows, cols) in enumerate(loads):
                if i >= ns:
                    c, v = freed[i - ns]
                    sync.wait_ge(c.sem, v)
                dst = buf[:, i % ns, :][0:rows, 0:cols]
                sync.dma_start(out=dst, in_=src).then_inc(sc[i % ns].sem, 16)
        for i in range(len(loads)):
            sc[i % ns].n += 16


_GUARD_OPS = {
    "tensor_tensor": 1, "tensor_scalar": 1, "scalar_tensor_tensor": 1,
    "tensor_copy": 1, "reduce_sum": 1, "reduce_max": 1, "reciprocal": 1,
    "tensor_tensor_scan": 1, "activation": 1, "copy": 1, "add": 1, "mul": 1,
    "memset": 0, "tensor_scalar_add": 1, "tensor_scalar_mul": 1, "tensor_sub": 1,
    "tensor_add": 1, "tensor_mul": 1, "tensor_max": 1, "select": 1,
}


def _ap_name(x):
    t = getattr(x, "tensor", None)
    if t is not None:
        return getattr(t, "name", None)
    return getattr(x, "name", None)


class Guard:

    def __init__(self, eng):
        self._e = eng
        self._w = set()
        self._r = set()

    def drain(self):
        self._w.clear()
        self._r.clear()
        return self._e.drain()

    def __getattr__(self, m):
        e = self._e
        f = getattr(e, m)
        if m not in _GUARD_OPS:
            return f
        n_in_from = _GUARD_OPS[m]

        def call(*args, **kw):
            ins = []
            outs = []
            for i, a in enumerate(args):
                nm = _ap_name(a)
                if nm is None:
                    continue
                (outs if i < n_in_from else ins).append(nm)
            for kk, a in kw.items():
                nm = _ap_name(a)
                if nm is None:
                    continue
                (outs if kk == "accum_out" else ins).append(nm)
            if (any(nm in self._w for nm in ins)
                    or any(nm in self._w or nm in self._r for nm in outs)):
                e.drain()
                self._w.clear()
                self._r.clear()
            r = f(*args, **kw)
            self._w.update(outs)
            self._r.update(ins)
            return r
        return call


class Pipe:

    def __init__(self, k: K):
        self.k = k
        self.jobs = []
        self.rings = []

    def job(self, pe=None, nb=0, steps=(), after=(), wtiles=None, banks_from=None):
        norm = []
        for s in (tuple(x) for x in steps):
            if len(s) == 2:
                norm.append((s[0], s[1], 1, None))
            elif len(s) == 3:
                norm.append((s[0], s[1], s[2], None))
            else:
                norm.append(s)
        self.jobs.append(dict(pe=pe, nb=nb, steps=norm, after=list(after), wtiles=wtiles,
                              banks_from=banks_from))
        return len(self.jobs) - 1

    def run(self, block):
        k = self.k
        pe_ctr = k.ctr("pe")
        for ji, j in enumerate(self.jobs):
            j["bwaits"] = ()
            if j["banks_from"] is not None:
                src_j = self.jobs[j["banks_from"]]
                j["banks"] = src_j["banks"]
                j["bank_ids"] = src_j["bank_ids"]
            elif j["nb"]:
                ids, waits = k.banks.alloc(j["nb"])
                j["banks"] = [k.banks.t[b][:] for b in ids]
                j["bank_ids"] = ids
                j["bwaits"] = waits
            else:
                j["banks"] = None
            if j["pe"] is not None:
                pe_ctr.n += 1
                j["pe_val"] = pe_ctr.n
            j["step_vals"] = []
            for eng, fn, inc, cust in j["steps"]:
                c = cust if cust is not None else k.ctr("st_" + eng)
                c.n += inc
                j["step_vals"].append((c, c.n))
            if j["nb"] or j["banks_from"] is not None:
                if j["steps"]:
                    k.banks.set_rel(j["bank_ids"], j["step_vals"][-1])
                else:
                    k.banks.set_rel(j["bank_ids"], (pe_ctr, j["pe_val"]))
            if j["wtiles"] is not None:
                ring, idxs = j["wtiles"]
                tgt = (pe_ctr, j["pe_val"]) if j["pe"] is not None else j["step_vals"][-1]
                for ii in idxs:
                    ring.freed[ii] = tgt
        for j in self.jobs:
            aw = []
            for a in j["after"]:
                if a[0] == "ctr":
                    aw.append((a[1], a[2]))
                    continue
                ji, stage = a
                tj = self.jobs[ji]
                aw.append((pe_ctr, tj["pe_val"]) if stage == "pe" else tj["step_vals"][stage])
            j["awaits"] = aw

        def emit_waits(eng, waits):
            best = {}
            for c, v in waits:
                if id(c) not in best or v > best[id(c)][1]:
                    best[id(c)] = (c, v)
            for c, v in best.values():
                eng.wait_ge(c.sem, v)

        for r in self.rings:
            r.emit_loader(block)
        jobs = self.jobs

        @block.tensor
        def _(te):
            for j in jobs:
                if j["pe"] is None:
                    continue
                emit_waits(te, list(j["bwaits"]) + list(j["awaits"]))
                j["pe"](te, j["banks"]).then_inc(pe_ctr.sem, 1)

        used = []
        for j in jobs:
            for eng, _, _, _ in j["steps"]:
                if eng not in used:
                    used.append(eng)
        for eng_name in used:
            def emit(eng, eng_name=eng_name):
                for j in jobs:
                    for si, (en, fn, inc, cust) in enumerate(j["steps"]):
                        if en != eng_name:
                            continue
                        waits = []
                        if si == 0:
                            if j["pe"] is not None:
                                waits.append((pe_ctr, j["pe_val"]))
                            waits.extend(j["awaits"])
                        else:
                            waits.append(j["step_vals"][si - 1])
                        emit_waits(eng, waits)
                        c, v = j["step_vals"][si]
                        fn(Guard(eng), j["banks"]).then_inc(c.sem, inc)
            getattr(block, eng_name)(emit)


def _prep(inputs):
    f = np.float32
    ii = {k: np.asarray(v) for k, v in inputs.items()}
    a = {}
    a["inprojT"] = np.ascontiguousarray(ii["L0_attn_in_proj_w"].T.astype(f))
    cw = np.zeros(((NXBC + 1) * 128, 4), f)
    cw[:2304] = ii["L0_attn_conv_w"]
    a["convw0"] = np.ascontiguousarray(cw.reshape(NXBC + 1, 128, 4).transpose(1, 2, 0))
    cb = np.zeros(((NXBC + 1) * 128,), f)
    cb[:2304] = ii["L0_attn_conv_b"]
    a["convb0"] = np.ascontiguousarray(cb.reshape(NXBC + 1, 128).T)
    a["dtb0"] = ii["L0_attn_dt_bias"].reshape(NH, 1).astype(f)
    a["alog0"] = ii["L0_attn_A_log"].reshape(NH, 1).astype(f)
    a["d0row"] = ii["L0_attn_D"].reshape(1, NH).astype(f)
    a["normw0"] = np.ascontiguousarray(np.broadcast_to(ii["L0_attn_norm_w"], (128, D)).astype(f))
    a["ssmw0"] = np.ascontiguousarray(np.broadcast_to(ii["L0_attn_ssm_norm_w"], (128, DIN)).astype(f))
    a["outprojT"] = np.ascontiguousarray(ii["L0_attn_out_proj_w"].T.astype(f))
    a["gw0"] = np.ascontiguousarray(ii["L0_ffn_gate_w"].astype(f))
    a["w_gate0"] = np.ascontiguousarray(ii["L0_ffn_w_gate"].astype(f))
    a["w_up0"] = np.ascontiguousarray(ii["L0_ffn_w_up"].astype(f))
    a["w_down0"] = np.ascontiguousarray(ii["L0_ffn_w_down"].astype(f))
    a["ws_gate0"] = np.ascontiguousarray(ii["L0_ffn_ws_gate"].astype(f))
    a["ws_up0"] = np.ascontiguousarray(ii["L0_ffn_ws_up"].astype(f))
    a["ws_down0"] = np.ascontiguousarray(ii["L0_ffn_ws_down"].astype(f))
    for nm, key in [("ln1g", "L1_attn_ln1_g"), ("ln1b", "L1_attn_ln1_b"),
                    ("ln2g", "L1_attn_ln2_g"), ("ln2b", "L1_attn_ln2_b"),
                    ("rmsin", "L1_attn_rms_in_w"), ("b2v", "L1_attn_b2")]:
        a[nm] = np.ascontiguousarray(np.broadcast_to(ii[key], (128, D)).astype(f))
    a["onorm"] = np.ascontiguousarray(np.broadcast_to(ii["L1_attn_o_norm_w"], (128, VD)).astype(f))
    a["dtb1bc"] = np.ascontiguousarray(np.broadcast_to(ii["L1_attn_dt_bias"], (128, H1)).astype(f))
    a["wq"] = np.ascontiguousarray(ii["L1_attn_Wq"].astype(f))
    a["wk"] = np.ascontiguousarray(ii["L1_attn_Wk"].astype(f))
    a["wv"] = np.ascontiguousarray(ii["L1_attn_Wv"].astype(f))
    a["wa"] = np.ascontiguousarray(ii["L1_attn_Wa"].astype(f))
    a["wb"] = np.ascontiguousarray(ii["L1_attn_Wb"].astype(f))
    a["wg1"] = np.ascontiguousarray(ii["L1_attn_Wg"].astype(f))
    a["wo"] = np.ascontiguousarray(ii["L1_attn_Wo"].astype(f))
    a["w2m"] = np.ascontiguousarray(ii["L1_attn_W2"].astype(f))
    a["cwq"] = np.ascontiguousarray(ii["L1_attn_conv_q_w"].reshape(8, 128, 4).transpose(1, 2, 0).astype(f))
    a["cwk"] = np.ascontiguousarray(ii["L1_attn_conv_k_w"].reshape(8, 128, 4).transpose(1, 2, 0).astype(f))
    a["cwv"] = np.ascontiguousarray(ii["L1_attn_conv_v_w"].reshape(16, 128, 4).transpose(1, 2, 0).astype(f))
    a["dtb1"] = ii["L1_attn_dt_bias"].reshape(H1, 1).astype(f)
    a["alog1"] = ii["L1_attn_A_log"].reshape(H1, 1).astype(f)
    a["gw1"] = np.ascontiguousarray(ii["L1_ffn_gate_w"].T.astype(f))
    a["rmsw1"] = np.ascontiguousarray(np.broadcast_to(ii["L1_ffn_rms_w"], (128, D)).astype(f))
    a["f1w1"] = np.ascontiguousarray(ii["L1_ffn_W1"].transpose(0, 2, 1).astype(f))
    a["f1w2"] = np.ascontiguousarray(ii["L1_ffn_W2"].transpose(0, 2, 1).astype(f))
    a["f1w3"] = np.ascontiguousarray(ii["L1_ffn_W3"].transpose(0, 2, 1).astype(f))
    a["ident"] = np.eye(128, dtype=f)
    a["ones128"] = np.ones((1, 128), f)
    a["onescol"] = np.ones((128, 1), f)
    st = np.arange(128)
    a["m_up_strict"] = (st[:, None] < st[None, :]).astype(f)
    a["iotabc"] = np.ascontiguousarray(np.broadcast_to(st.astype(f), (128, 128)))
    a["iotacol"] = st.reshape(128, 1).astype(f)
    a["m_up_incl"] = (st[:, None] <= st[None, :]).astype(f)
    a["m_lo_incl"] = (st[:, None] >= st[None, :]).astype(f)
    a["m_lo_strict"] = (st[:, None] > st[None, :]).astype(f)
    a["epsc"] = np.ascontiguousarray(np.broadcast_to(np.array([1e-4, 1e-5, 1e-6, 0.0], f), (128, 4)))
    a["bo32"] = np.kron(np.eye(NH, dtype=f), np.ones((1, CH), f))
    a["bo4"] = np.kron(np.eye(H1, dtype=f), np.ones((1, CH), f))
    return a


WSPECS = {
    "inprojT": [D, DPROJ], "convw0": [128, 4, NXBC + 1], "convb0": [128, NXBC + 1],
    "dtb0": [NH, 1], "alog0": [NH, 1], "d0row": [1, NH],
    "normw0": [128, D], "ssmw0": [128, DIN], "outprojT": [DIN, D],
    "gw0": [D, 8], "w_gate0": [8, D, DE0], "w_up0": [8, D, DE0],
    "w_down0": [8, DE0, D], "ws_gate0": [D, DE0], "ws_up0": [D, DE0], "ws_down0": [DE0, D],
    "ln1g": [128, D], "ln1b": [128, D], "ln2g": [128, D], "ln2b": [128, D],
    "rmsin": [128, D], "b2v": [128, D], "onorm": [128, VD], "dtb1bc": [128, H1],
    "wq": [D, D], "wk": [D, D], "wv": [D, VAL := 2048], "wa": [D, H1], "wb": [D, H1],
    "wg1": [D, 2048], "wo": [2048, D], "w2m": [D, D],
    "cwq": [128, 4, 8], "cwk": [128, 4, 8], "cwv": [128, 4, 16],
    "dtb1": [H1, 1], "alog1": [H1, 1],
    "gw1": [D, 8], "rmsw1": [128, D],
    "f1w1": [8, D, DE1], "f1w2": [8, D, DE1], "f1w3": [8, DE1, D],
    "ident": [128, 128], "ones128": [1, 128], "onescol": [128, 1],
    "m_up_incl": [128, 128], "m_lo_incl": [128, 128], "m_lo_strict": [128, 128],
    "epsc": [128, 4], "m_up_strict": [128, 128], "iotabc": [128, 128], "iotacol": [128, 1],
    "bo32": [NH, NH * CH], "bo4": [H1, H1 * CH],
}


def emit_ln(pipe, xin, xout, stats, sq, epsb, gb=None, after=()):
    def s_sc(sc, banks):
        return sc.activation(sq, xin, AF.Square, accum_out=stats[:, 1:2])

    def s_ve(ve, banks):
        ve.reduce_sum(stats[:, 0:1], xin, axis=AX.X)
        ve.tensor_scalar(stats[:, 2:3], stats[:, 0:1], 1.0 / D, None, ALU.mult)
        ve.tensor_scalar(stats[:, 3:4], stats[:, 1:2], 1.0 / D, None, ALU.mult)
        ve.tensor_tensor(stats[:, 4:5], stats[:, 2:3], stats[:, 2:3], ALU.mult)
        ve.tensor_tensor(stats[:, 3:4], stats[:, 3:4], stats[:, 4:5], ALU.subtract)
        return ve.tensor_scalar(stats[:, 5:6], stats[:, 2:3], -1.0, None, ALU.mult)

    def s_sc2(sc, banks):
        return sc.activation(stats[:, 6:7], stats[:, 3:4], AF.Sqrt, bias=epsb)

    def s_ve2(ve, banks):
        ve.reciprocal(stats[:, 7:8], stats[:, 6:7])
        r = ve.tensor_scalar(xout, xin, stats[:, 5:6], stats[:, 7:8], ALU.add, ALU.mult)
        if gb is not None:
            g, b = gb
            ve.tensor_tensor(xout, xout, g, ALU.mult)
            r = ve.tensor_tensor(xout, xout, b, ALU.add)
        return r

    return pipe.job(steps=[("scalar", s_sc), ("vector", s_ve), ("scalar", s_sc2),
                           ("vector", s_ve2)], after=list(after))


def emit_rms(pipe, xin, xout, stats, sq, epsb, scale, wbc=None, after=(), mean=True):
    def s_sc(sc, banks):
        sc.activation(sq, xin, AF.Square, accum_out=stats[:, 0:1])
        sc.drain()
        return sc.activation(stats[:, 1:2], stats[:, 0:1], AF.Sqrt, scale=scale, bias=epsb)

    def s_ve(ve, banks):
        ve.reciprocal(stats[:, 2:3], stats[:, 1:2])
        if wbc is not None:
            return ve.scalar_tensor_tensor(xout, xin, stats[:, 2:3], wbc, ALU.mult, ALU.mult)
        return ve.tensor_scalar(xout, xin, stats[:, 2:3], None, ALU.mult)

    return pipe.job(steps=[("scalar", s_sc), ("vector", s_ve)], after=list(after))


def _conv_taps(ve, raw3, nch, off, wbuf, cvout, tmp, bias=None):
    o3 = cvout[:]
    t3 = tmp[:]
    r = None
    for i in range(4):
        wbc = wbuf[:, i, 0:nch].unsqueeze(2).broadcast_to((128, nch, CH))
        xin = raw3[:, 0:nch, off + i: off + i + CH]
        if i == 0:
            r = ve.tensor_tensor(o3, xin, wbc, ALU.mult)
        else:
            ve.tensor_tensor(t3, xin, wbc, ALU.mult)
            r = ve.tensor_tensor(o3, o3, t3, ALU.add)
    if bias is not None:
        bbc = bias[:, 0:nch].unsqueeze(2).broadcast_to((128, nch, CH))
        r = ve.tensor_tensor(o3, o3, bbc, ALU.add)
    return r


def _l0_mamba(k: K, dr, x_res, seq, g):
    nc = k.nc
    es2 = contextlib.ExitStack()
    sb = lambda n, s: es2.enter_context(nc.sbuf_tensor(k.fresh(n), list(s), F32))
    ident, ones1, m_up = g["ident"], g["ones1"], g["m_up"]
    bo32 = sb("bo32", [NH, NH * CH])
    convw = sb("convw", [128, 4, NXBC + 1])
    convb = sb("convb", [128, NXBC + 1])
    normw = sb("normw", [128, D])
    ssmw = sb("ssmw", [128, DIN])
    uT = sb("uT", [128, 8, CH])
    ubuf = sb("ubuf", [128, D])
    zb = sb("zb", [128, DIN])
    xbc_raw = sb("xbcr", [128, NXBC + 1, CH + 3])
    xbc_cv = sb("xbccv", [128, NXBC, CH])
    cvtmp = sb("cvtmp", [128, NXBC, CH])
    yb = sb("yb", [128, DIN])
    ynT = sb("ynT", [128, 16, CH])
    hst = sb("hst", [128, NH * HD])
    mst = sb("mst", [128, 16 * CH])
    gst = sb("gst", [128, CH])
    bmtm = sb("bmtm", [128, CH])
    sp_sm = sb("spsm", [NH, CH])
    a_sm = sb("asm", [NH, CH])
    cum_sm = sb("cumsm", [NH, CH])
    ncum_sm = sb("ncumsm", [NH, CH])
    wfm_sm = sb("wfmsm", [NH, CH])
    cend = sb("cend", [NH, 1])
    cendrow = sb("cendrow", [1, NH])
    cum_tm = sb("cumtm", [128, NH])
    expc_tm = sb("expctm", [128, NH])
    w_tm = sb("wtm", [128, NH])
    dt_tm = sb("dttm", [128, NH])
    dtw_tm = sb("dtwtm", [128, NH])
    dend = sb("dend", [128, NH])
    dtx = sb("dtx", [128, CH])
    dtxw = sb("dtxw", [128, CH])
    xstm = sb("xstm", [128, CH])
    s64 = sb("s64", [128, 128])
    xsum = sb("xsum", [128, D])
    nst = sb("nst", [128, 8])
    nsq = sb("nsq", [128, DIN])
    pst = sb("pst", [128, 8])
    ring = WRing(k, k.fresh("wrA"), 8, sb=sb)
    ld = k.ctr("ld0")

    with nc.Block() as b:
        @b.sync
        def _(sync):
            for dst, src in [(convw[:], dr["convw0"]), (convb[:], dr["convb0"]),
                             (normw[:], dr["normw0"]), (ssmw[:], dr["ssmw0"]),
                             (bo32[:], dr["bo32"])]:
                sync.dma_start(out=dst, in_=src).then_inc(ld.sem, 16)
                ld.n += 16
            sync.wait_ge(ld.sem, ld.n)
    with nc.Block() as b:
        @b.vector
        def _(ve):
            ve.memset(hst[:], 0.0)
            ve.memset(xbc_raw[:, :, 0:3], 0.0)

    for c in range(NT):
        pipe = Pipe(k)
        pipe.rings = [ring]
        ring.begin()
        xt = x_res[:, c, :]

        jprev_pairs = None

        if c > 0:
            def f_halo(ve, banks):
                return ve.tensor_copy(xbc_raw[:, 0:NXBC, 0:3], xbc_raw[:, 0:NXBC, CH:CH + 3])
            j_halo = pipe.job(steps=[("vector", f_halo)])
        else:
            j_halo = None

        j_nrm = emit_rms(pipe, xt, ubuf[:], nst, nsq[:, 0:D], g["epsc"][:, 0:1], 1.0 / D, wbc=normw[:])

        jtr = []
        for dc in range(8):
            def f_pe(te, banks, dc=dc):
                return te.transpose(banks[0][:, 0:128], ubuf[:, dc * 128:(dc + 1) * 128], ident[:])

            def f_ev(ve, banks, dc=dc):
                return ve.tensor_copy(uT[:, dc, :], banks[0][:, 0:128])
            jtr.append(pipe.job(pe=f_pe, nb=1, steps=[("vector", f_ev)], after=[(j_nrm, 1)]))

        jz = []
        for fz in range(4):
            idxs = [ring.add(dr["inprojT"][dc * 128:(dc + 1) * 128, fz * 512:(fz + 1) * 512], 128, 512)
                    for dc in range(8)]

            def f_pe(te, banks, idxs=idxs):
                last = None
                for j in range(8):
                    ring.wait(te, idxs[j])
                    last = te.matmul(banks[0][:, 0:512], uT[:, j, :], ring.tile(idxs[j]),
                                     start=(j == 0), stop=(j == 7))
                return last

            def f_ev(sc, banks, fz=fz):
                return sc.copy(zb[:, fz * 512:(fz + 1) * 512], banks[0][:, 0:512])
            jz.append(pipe.job(pe=f_pe, nb=1, steps=[("scalar", f_ev)], after=[(jtr[-1], 0)],
                               wtiles=(ring, idxs)))

        jx = []
        for grp in range(5):
            fcs = list(range(grp * 4, min(grp * 4 + 4, NXBC + 1)))
            nfc = len(fcs)
            gcols = sum(128 if fc < NXBC else NH for fc in fcs)
            idxs = [ring.add(dr["inprojT"][dc * 128:(dc + 1) * 128,
                                           DIN + fcs[0] * 128: DIN + fcs[0] * 128 + gcols], 128, gcols)
                    for dc in range(8)]

            def f_pe(te, banks, fcs=fcs, nfc=nfc, idxs=idxs):
                last = None
                for j in range(8):
                    ring.wait(te, idxs[j])
                    tl = ring.tile(idxs[j])
                    for i, fc in enumerate(fcs):
                        rows = 128 if fc < NXBC else NH
                        last = te.matmul(banks[i][0:rows, 0:CH], tl[:, i * 128:i * 128 + rows],
                                         uT[:, j, :], start=(j == 0), stop=(j == 7))
                return last

            aft = [(jtr[-1], 0)]
            if j_halo is not None and grp == 0:
                aft.append((j_halo, 0))

            def f_ev(ve, banks, fcs=fcs):
                last = None
                for i, fc in enumerate(fcs):
                    rows = 128 if fc < NXBC else NH
                    last = ve.tensor_copy(xbc_raw[0:rows, fc, 3:3 + CH], banks[i][0:rows, 0:CH])
                return last
            jx.append(pipe.job(pe=f_pe, nb=nfc, steps=[("vector", f_ev)], after=aft,
                               wtiles=(ring, idxs)))
        j_xbc = jx[-1]

        def f_cv(ve, banks):
            return _conv_taps(ve, xbc_raw, NXBC, 0, convw, xbc_cv, cvtmp, bias=convb)

        def f_cv_sc(sc, banks):
            return sc.activation(xbc_cv[:], xbc_cv[:], AF.Silu)
        j_conv = pipe.job(steps=[("vector", f_cv), ("scalar", f_cv_sc)], after=[(j_xbc, 0)])

        def f_dt_sc(sc, banks):
            dtraw = xbc_raw[0:NH, NXBC, 3:3 + CH]
            sc.activation(sp_sm[:], dtraw, AF.Exp, bias=g["dtb0"][:])
            sc.drain()
            return sc.activation(sp_sm[:], sp_sm[:], AF.Ln, bias=1.0)

        def f_dt_ve(ve, banks):
            ve.tensor_scalar(a_sm[:], sp_sm[:], g["negea0"][:], None, ALU.mult)
            ve.tensor_tensor_scan(cum_sm[:], a_sm[:], g["zeros"][0:NH, :], 0.0, ALU.add, ALU.add)
            ve.reduce_sum(cend[:], a_sm[:], axis=AX.X)
            ve.tensor_scalar(ncum_sm[:], cum_sm[:], -1.0, None, ALU.mult)
            return ve.tensor_scalar(wfm_sm[:], cum_sm[:], cend[:], None, ALU.subtract)

        def f_dt_sc2(sc, banks):
            return sc.activation(wfm_sm[:], wfm_sm[:], AF.Exp, scale=-1.0)
        j_dt = pipe.job(steps=[("scalar", f_dt_sc), ("vector", f_dt_ve),
                               ("scalar", f_dt_sc2)], after=[(j_xbc, 0)])

        def f_smt(te, banks):
            te.matmul(banks[0][0:128, 0:NH], cum_sm[:], ident[0:NH, 0:NH],
                      start=True, stop=False, is_transpose=True)
            te.matmul(banks[0][0:1, 3 * NH:3 * NH + NH], cend[:], ident[0:NH, 0:NH],
                      start=False, stop=False, is_transpose=True)
            te.matmul(banks[0][0:128, NH:2 * NH], wfm_sm[:], ident[0:NH, 0:NH],
                      start=False, stop=False, is_transpose=True)
            return te.matmul(banks[0][0:128, 2 * NH:3 * NH], sp_sm[:], ident[0:NH, 0:NH],
                             start=False, stop=True, is_transpose=True)

        def f_smt_ve(ve, banks):
            ve.tensor_copy(cum_tm[:], banks[0][:, 0:NH])
            ve.tensor_copy(w_tm[:], banks[0][:, NH:2 * NH])
            ve.tensor_copy(dt_tm[:], banks[0][:, 2 * NH:3 * NH])
            ve.tensor_copy(cendrow[:], banks[0][0:1, 3 * NH:4 * NH])
            return ve.tensor_tensor(dtw_tm[:], dt_tm[:], w_tm[:], ALU.mult)

        def f_smt_sc(sc, banks):
            return sc.activation(expc_tm[:], cum_tm[:], AF.Exp)
        j_smt = pipe.job(pe=f_smt, nb=1, steps=[("vector", f_smt_ve), ("scalar", f_smt_sc)],
                         after=[(j_dt, 2)])

        def f_dend(te, banks):
            return te.matmul(banks[0][:, 0:NH], ones1[:], cendrow[:], start=True, stop=True)

        def f_dend_sc(sc, banks):
            return sc.activation(dend[:], banks[0][:, 0:NH], AF.Exp)
        j_dend = pipe.job(pe=f_dend, nb=1, steps=[("scalar", f_dend_sc)], after=[(j_smt, 0)])

        def f_g(te, banks):
            return te.matmul(banks[0][:, 0:CH], xbc_cv[:, 16, :], xbc_cv[:, 17, :],
                             start=True, stop=True)

        def f_g_ve(ve, banks):
            return ve.tensor_tensor(gst[:], banks[0][:, 0:CH], m_up[:], ALU.mult)
        j_g = pipe.job(pe=f_g, nb=1, steps=[("vector", f_g_ve)], after=[(j_conv, 1)])

        def f_bt(te, banks):
            return te.transpose(banks[0][:, 0:CH], xbc_cv[:, 16, :], ident[:])

        def f_bt_ve(ve, banks):
            return ve.tensor_copy(bmtm[:], banks[0][:, 0:CH])
        j_bt = pipe.job(pe=f_bt, nb=1, steps=[("vector", f_bt_ve)], after=[(j_conv, 1)])

        j_ps_last = None
        for half in range(2):
            def f_mst(te, banks, half=half):
                last = None
                for j in range(16):
                    q, cc = j // 4, (j % 4) * 128
                    sel = bo32[:, (half * 16 + j) * CH:(half * 16 + j + 1) * CH]
                    te.matmul(banks[q][:, cc:cc + 128], sel, cum_sm[:],
                              start=(j % 4 == 0), stop=False)
                    last = te.matmul(banks[q][:, cc:cc + 128], ncum_sm[:], sel,
                                     start=False, stop=(j % 4 == 3))
                return last

            def f_mst_ve(ve, banks):
                last = None
                for q in range(4):
                    last = ve.tensor_scalar(mst[:, q * 512:(q + 1) * 512], banks[q][:, 0:512],
                                            0.0, None, ALU.min)
                return last

            def f_mst_sc(sc, banks):
                return sc.activation(mst[:], mst[:], AF.Exp)

            def f_mst_ve2(ve, banks):
                g3 = gst[:].unsqueeze(1).broadcast_to((128, 16, CH))
                m3 = mst[:].rearrange("p (a b) -> p a b", a=16)
                return ve.tensor_tensor(m3, m3, g3, ALU.mult)
            aft = [(j_dt, 1), (j_g, 0)]
            if j_ps_last is not None:
                aft.append((j_ps_last, "pe"))
            j_mst = pipe.job(pe=f_mst, nb=4, steps=[("vector", f_mst_ve),
                                                    ("scalar", f_mst_sc),
                                                    ("vector", f_mst_ve2)], after=aft)

            for pp in range(8):
                p = half * 8 + pp
                def f_pt(te, banks, p=p):
                    return te.transpose(banks[0][:, 0:CH], xbc_cv[:, p, :], ident[:])

                def f_pt_ve(ve, banks, p=p):
                    for hh, col in ((2 * p, 0), (2 * p + 1, 64)):
                        ve.tensor_scalar(dtx[:, col:col + 64], banks[0][:, col:col + 64],
                                         dt_tm[:, hh:hh + 1], None, ALU.mult)
                        ve.tensor_scalar(dtxw[:, col:col + 64], banks[0][:, col:col + 64],
                                         dtw_tm[:, hh:hh + 1], None, ALU.mult)
                    return ve.tensor_copy(xstm[:], banks[0][:, 0:CH])
                j_pt = pipe.job(pe=f_pt, nb=1, steps=[("vector", f_pt_ve)],
                                after=[(j_conv, 1), (j_smt, 0)])

                def f_py(te, banks, p=p, pp=pp):
                    te.matmul(banks[0][:, 0:CH], xbc_cv[:, 17, :], hst[:, p * 128:(p + 1) * 128],
                              start=True, stop=True)
                    i0 = 2 * pp
                    te.matmul(banks[1][:, 0:64], mst[:, i0 * CH:(i0 + 1) * CH], dtx[:, 0:64],
                              start=True, stop=False)
                    return te.matmul(banks[1][:, 64:128], mst[:, (i0 + 1) * CH:(i0 + 2) * CH],
                                     dtx[:, 64:128], start=False, stop=True)

                def f_py_ve(ve, banks, p=p):
                    last = None
                    for hh, col in ((2 * p, 0), (2 * p + 1, 64)):
                        ve.tensor_scalar(s64[:, 0:64], banks[0][:, col:col + 64],
                                         expc_tm[:, hh:hh + 1], None, ALU.mult)
                        ve.scalar_tensor_tensor(s64[:, 64:128], xstm[:, col:col + 64],
                                                g["d0bc"][:, hh:hh + 1], banks[1][:, col:col + 64],
                                                ALU.mult, ALU.add)
                        last = ve.tensor_tensor(yb[:, hh * 64:(hh + 1) * 64],
                                                s64[:, 0:64], s64[:, 64:128], ALU.add)
                    return last
                j_py = pipe.job(pe=f_py, nb=2, steps=[("vector", f_py_ve)],
                                after=[(j_pt, 0), (j_mst, 2), (j_dend, 0)])

                def f_ps(te, banks):
                    return te.matmul(banks[0][:, 0:CH], bmtm[:], dtxw[:], start=True, stop=True)

                def f_ps_ve(ve, banks, p=p):
                    last = None
                    for hh, col in ((2 * p, 0), (2 * p + 1, 64)):
                        last = ve.scalar_tensor_tensor(hst[:, hh * 64:(hh + 1) * 64],
                                                       hst[:, hh * 64:(hh + 1) * 64],
                                                       dend[:, hh:hh + 1], banks[0][:, col:col + 64],
                                                       ALU.mult, ALU.add)
                    return last
                j_ps_last = pipe.job(pe=f_ps, nb=1, steps=[("vector", f_ps_ve)],
                                     after=[(j_bt, 0), (j_py, "pe")])

        def f_g1(sc, banks):
            return sc.activation(zb[:], zb[:], AF.Silu)

        def f_g2(ve, banks):
            return ve.tensor_tensor(yb[:], yb[:], zb[:], ALU.mult)
        j_gate = pipe.job(steps=[("scalar", f_g1), ("vector", f_g2)],
                          after=[(j_ps_last, 0), (jz[-1], 0)])
        j_ssn = emit_rms(pipe, yb[:], yb[:], nst, nsq[:], g["epsc"][:, 1:2], 1.0 / DIN, wbc=ssmw[:],
                         after=[(j_gate, 1)])

        jyt = []
        for ec in range(16):
            def f_pe(te, banks, ec=ec):
                return te.transpose(banks[0][:, 0:128], yb[:, ec * 128:(ec + 1) * 128], ident[:])

            def f_ev(ve, banks, ec=ec):
                return ve.tensor_copy(ynT[:, ec, :], banks[0][:, 0:128])
            jyt.append(pipe.job(pe=f_pe, nb=1, steps=[("vector", f_ev)], after=[(j_ssn, 1)]))

        jop = []
        for fo in range(2):
            idxs = [ring.add(dr["outprojT"][ec * 128:(ec + 1) * 128, fo * 512:(fo + 1) * 512], 128, 512)
                    for ec in range(16)]

            def f_pe1(te, banks, idxs=idxs):
                last = None
                for j in range(8):
                    ring.wait(te, idxs[j])
                    last = te.matmul(banks[0][:, 0:512], ynT[:, j, :], ring.tile(idxs[j]),
                                     start=(j == 0), stop=False)
                return last

            def f_pe2(te, banks, idxs=idxs):
                last = None
                for j in range(8, 16):
                    ring.wait(te, idxs[j])
                    last = te.matmul(banks[0][:, 0:512], ynT[:, j, :], ring.tile(idxs[j]),
                                     start=False, stop=(j == 15))
                return last

            def f_ev(ve, banks, fo=fo):
                return ve.scalar_tensor_tensor(xsum[:, fo * 512:(fo + 1) * 512],
                                               x_res[:, c, fo * 512:(fo + 1) * 512],
                                               2.0, banks[0][:, 0:512], ALU.mult, ALU.add)
            ja = pipe.job(pe=f_pe1, nb=1, after=[(jyt[-1], 0)], wtiles=(ring, idxs[:8]))
            jop.append(pipe.job(pe=f_pe2, banks_from=ja, steps=[("vector", f_ev)],
                                wtiles=(ring, idxs[8:])))

        emit_ln(pipe, xsum[:], x_res[:, c, :], pst, nsq[:, 0:D], g["epsc"][:, 1:2], after=[(jop[-1], 0)])

        with nc.Block() as blk:
            pipe.run(blk)
    es2.close()


def _moe(k: K, dr, x_res, seq, layer, g):
    nc = k.nc
    es2 = contextlib.ExitStack()
    sb = lambda n, s: es2.enter_context(nc.sbuf_tensor(k.fresh(n), list(s), F32))
    ident = g["ident"]
    gw = g["gw0"] if layer == 0 else g["gw1"]
    DE = DE0 if layer == 0 else DE1
    NFC = DE // 128
    NGRP = (NFC + 3) // 4
    TPT = 4

    xT = sb("xT", [128, 8, TPT * 128])
    hid = sb("hid", [128, NFC, 512])
    gsil = sb("gsil", [128, 2, 4, 512])
    yacc = sb("yacc", [128, TPT, D])
    w8 = sb("w8", [128, TPT, 8])
    ybuf = sb("ybuf", [128, D]) if layer == 1 else None
    rmsw = sb("rmsw", [128, D]) if layer == 1 else None
    xsum = sb("xsum", [128, D])
    nsq = sb("nsq", [128, D])
    rsts = [sb("rst", [128, 8]) for _ in range(TPT)]
    scrs = [sb("scr", [128, 8]) for _ in range(TPT)]
    mb1s = [sb("mb1", [128, 8]) for _ in range(TPT)]
    mb2s = [sb("mb2", [128, 8]) for _ in range(TPT)]
    psts = [sb("pstf", [128, 8]) for _ in range(TPT)]
    nsts = [sb("nstf", [128, 8]) for _ in range(TPT)]
    ring = WRing(k, k.fresh("wrF"), 15, sb=sb)
    ld = k.ctr("ld0")

    if layer == 1:
        with nc.Block() as b:
            @b.sync
            def _(sync):
                sync.dma_start(out=rmsw[:], in_=dr["rmsw1"]).then_inc(ld.sem, 16)
                ld.n += 16
                sync.wait_ge(ld.sem, ld.n)

    if layer == 0:
        experts = [("ws_gate0", "ws_up0", "ws_down0", None)] + \
                  [("w_gate0", "w_up0", "w_down0", e) for e in range(8)]
    else:
        experts = [("f1w1", "f1w2", "f1w3", e) for e in range(8)]

    for tile in range(NT // TPT):
        pipe = Pipe(k)
        pipe.rings = [ring]
        ring.begin()
        t0 = tile * TPT

        jtr_last = None
        jrt = []
        for tb in range(TPT):
            xt = x_res[:, t0 + tb, :]
            if layer == 1:
                aft_n = ([(jrt[tb - 1], 1), (jrt[tb - 1], 2), (jrt[tb - 1], "pe")]
                         if tb > 0 else [])
                jn = emit_rms(pipe, xt, ybuf[:], nsts[tb], nsq[:], g["epsc"][:, 2:3], 1.0 / D,
                              wbc=rmsw[:], after=aft_n)
                src = ybuf
                aft_tr = [(jn, 1)]
            else:
                src = None
                aft_tr = []
            for dc in range(8):
                sap = (xt[:, dc * 128:(dc + 1) * 128] if layer == 0
                       else ybuf[:, dc * 128:(dc + 1) * 128])

                def f_pe(te, banks, sap=sap):
                    return te.transpose(banks[0][:, 0:128], sap, ident[:])

                def f_ev(ve, banks, tb=tb, dc=dc):
                    return ve.tensor_copy(xT[:, dc, tb * 128:(tb + 1) * 128], banks[0][:, 0:128])
                jtr_last = pipe.job(pe=f_pe, nb=1, steps=[("vector", f_ev)], after=aft_tr)

            def f_lg(te, banks, tb=tb):
                last = None
                for dc in range(8):
                    last = te.matmul(banks[0][:, 0:8], xT[:, dc, tb * 128:(tb + 1) * 128],
                                     gw[:, dc, :], start=(dc == 0), stop=(dc == 7))
                return last

            rst, scr, mb1, mb2 = rsts[tb], scrs[tb], mb1s[tb], mb2s[tb]

            def f_r1(ve, banks):
                ve.reduce_max(rst[:, 0:1], banks[0][:, 0:8], axis=AX.X)
                return ve.tensor_scalar(rst[:, 1:2], rst[:, 0:1], -1.0, None, ALU.mult)

            def f_r2(sc, banks):
                return sc.activation(scr[:], banks[0][:, 0:8], AF.Exp, bias=rst[:, 1:2],
                                     accum_out=rst[:, 2:3])

            def f_r3(ve, banks, tb=tb):
                if layer == 0:
                    ve.reciprocal(rst[:, 3:4], rst[:, 2:3])
                    ve.tensor_scalar(scr[:], scr[:], rst[:, 3:4], None, ALU.mult)
                ve.reduce_max(rst[:, 4:5], scr[:], axis=AX.X)
                ve.tensor_scalar(mb1[:], scr[:], rst[:, 4:5], None, ALU.is_ge)
                ve.scalar_tensor_tensor(mb2[:], mb1[:], -1e30, scr[:], ALU.mult, ALU.add)
                ve.reduce_max(rst[:, 5:6], mb2[:], axis=AX.X)
                ve.tensor_scalar(mb2[:], mb2[:], rst[:, 5:6], None, ALU.is_ge)
                ve.tensor_tensor(rst[:, 6:7], rst[:, 4:5], rst[:, 5:6], ALU.add)
                if layer == 0:
                    ve.tensor_scalar(rst[:, 6:7], rst[:, 6:7], 1e-20, None, ALU.add)
                ve.reciprocal(rst[:, 7:8], rst[:, 6:7])
                ve.tensor_tensor(rst[:, 4:5], rst[:, 4:5], rst[:, 7:8], ALU.mult)
                ve.tensor_tensor(rst[:, 5:6], rst[:, 5:6], rst[:, 7:8], ALU.mult)
                ve.tensor_scalar(w8[:, tb, :], mb1[:], rst[:, 4:5], None, ALU.mult)
                return ve.scalar_tensor_tensor(w8[:, tb, :], mb2[:], rst[:, 5:6],
                                               w8[:, tb, :], ALU.mult, ALU.add)
            jrt.append(pipe.job(pe=f_lg, nb=1, steps=[("vector", f_r1), ("scalar", f_r2),
                                                      ("vector", f_r3)],
                                after=[(jtr_last, 0)]))

        j_dn_last = None
        for ei, (wg_nm, wu_nm, wd_nm, e) in enumerate(experts):
            wgd = dr[wg_nm] if e is None else dr[wg_nm][e]
            wud = dr[wu_nm] if e is None else dr[wu_nm][e]
            wdd = dr[wd_nm] if e is None else dr[wd_nm][e]
            jup = []
            for grp in range(NGRP):
                fcs = list(range(grp * 4, min(grp * 4 + 4, NFC)))
                nfc = len(fcs)
                for which, wsrc in (("g", wgd), ("u", wud)):
                    idxs = [ring.add(wsrc[dc * 128:(dc + 1) * 128,
                                          fcs[0] * 128: fcs[0] * 128 + nfc * 128], 128, nfc * 128)
                            for dc in range(8)]

                    def f_pe(te, banks, idxs=idxs, nfc=nfc):
                        last = None
                        for dc in range(8):
                            ring.wait(te, idxs[dc])
                            tl = ring.tile(idxs[dc])
                            for i in range(nfc):
                                last = te.matmul(banks[i][:, 0:512], tl[:, i * 128:(i + 1) * 128],
                                                 xT[:, dc, :],
                                                 start=(dc == 0), stop=(dc == 7))
                        return last

                    par = grp % 2
                    if which == "g":
                        def f_ev(sc, banks, nfc=nfc, par=par):
                            last = None
                            for i in range(nfc):
                                last = sc.activation(gsil[:, par, i, :], banks[i][:, 0:512], AF.Silu)
                            return last
                        aft = [(jtr_last, 0)]
                        if j_dn_last is not None and grp == 0:
                            aft.append((j_dn_last, 0))
                        if len(jup) >= 2:
                            aft.append((jup[-2], 0))
                        j_gt = pipe.job(pe=f_pe, nb=nfc, steps=[("scalar", f_ev)], after=aft,
                                        wtiles=(ring, idxs))
                    else:
                        def f_ev(ve, banks, fcs=fcs, nfc=nfc, par=par):
                            last = None
                            for i in range(nfc):
                                last = ve.tensor_tensor(hid[:, fcs[i], :], banks[i][:, 0:512],
                                                        gsil[:, par, i, :], ALU.mult)
                            return last
                        jup.append(pipe.job(pe=f_pe, nb=nfc, steps=[("vector", f_ev)],
                                            after=[(j_gt, 0)], wtiles=(ring, idxs)))
            for nd in range(2):
                idxs = [ring.add(wdd[ec * 128:(ec + 1) * 128, nd * 512:(nd + 1) * 512], 128, 512)
                        for ec in range(NFC)]

                def mk_pe(e0, e1, idxs=idxs):
                    def f_pe(te, banks):
                        last = None
                        for ec in range(e0, e1):
                            ring.wait(te, idxs[ec])
                            tl = ring.tile(idxs[ec])
                            for tb in range(TPT):
                                last = te.matmul(banks[tb][:, 0:512],
                                                 hid[:, ec, tb * 128:(tb + 1) * 128],
                                                 tl, start=(ec == 0), stop=(ec == NFC - 1))
                        return last
                    return f_pe

                def f_ev(ve, banks, nd=nd, ei=ei, e=e):
                    last = None
                    for tb in range(TPT):
                        dst = yacc[:, tb, nd * 512:(nd + 1) * 512]
                        if ei == 0:
                            if layer == 0:
                                last = ve.tensor_copy(dst, banks[tb][:, 0:512])
                            else:
                                last = ve.tensor_scalar(dst, banks[tb][:, 0:512],
                                                        w8[:, tb, e:e + 1], None, ALU.mult)
                        else:
                            last = ve.scalar_tensor_tensor(dst, banks[tb][:, 0:512],
                                                           w8[:, tb, e:e + 1], dst,
                                                           ALU.mult, ALU.add)
                    return last
                aft = [(jup[-1], 0)] + ([(jrt[-1], 2)] if ei <= 1 else [])
                bounds = list(range(0, NFC, 8)) + [NFC]
                jprev_dn = None
                for bi in range(len(bounds) - 1):
                    e0, e1 = bounds[bi], bounds[bi + 1]
                    lastsub = (bi == len(bounds) - 2)
                    j_dn_last = pipe.job(
                        pe=mk_pe(e0, e1), nb=(TPT if bi == 0 else 0),
                        banks_from=(None if bi == 0 else jprev_dn),
                        steps=([("vector", f_ev)] if lastsub else []),
                        after=(aft if bi == 0 else []),
                        wtiles=(ring, idxs[e0:e1]))
                    jprev_dn = j_dn_last

        jln_prev = None
        for tb in range(TPT):
            xt = x_res[:, t0 + tb, :]

            def f_x(ve, banks, tb=tb, xt=xt):
                if layer == 0:
                    return ve.tensor_tensor(xsum[:], xt, yacc[:, tb, :], ALU.add)
                return ve.scalar_tensor_tensor(xsum[:], xt, 2.0, yacc[:, tb, :],
                                               ALU.mult, ALU.add)
            aft_x = [(j_dn_last, 0)] + ([(jln_prev, 3), (jln_prev, 2)] if jln_prev is not None else [])
            j_x = pipe.job(steps=[("vector", f_x)], after=aft_x)
            jln_prev = emit_ln(pipe, xsum[:], xt, psts[tb], nsq[:], g["epsc"][:, 1:2],
                               after=[(j_x, 0)])

        with nc.Block() as blk:
            pipe.run(blk)
    es2.close()



def _conv_head(ve, raw, cwbuf, ch0, nch, out2d, tmp):
    o3 = out2d[:].rearrange("p (a b) -> p a b", a=nch)
    t3 = tmp[:, 0:nch, :]
    last = None
    for i in range(4):
        wbc = cwbuf[:, i, ch0:ch0 + nch].unsqueeze(2).broadcast_to((128, nch, CH))
        xin = raw[:, ch0:ch0 + nch, i:i + CH]
        if i == 0:
            last = ve.tensor_tensor(o3, xin, wbc, ALU.mult)
        else:
            ve.tensor_tensor(t3, xin, wbc, ALU.mult)
            last = ve.tensor_tensor(o3, o3, t3, ALU.add)
    return last


def _l1_delta(k: K, dr, x_res, seq, g):
    nc = k.nc
    es2 = contextlib.ExitStack()
    sb = lambda n, s: es2.enter_context(nc.sbuf_tensor(k.fresh(n), list(s), F32))
    ident, ones1 = g["ident"], g["ones1"]
    m_up, m_lo_s = g["m_up"], g["m_lo_s"]
    onescol = g["onescol"]
    bo4 = sb("bo4", [H1, H1 * CH])
    lnc0 = sb("lnc0", [128, D])
    lnc1 = sb("lnc1", [128, D])
    lnc2 = sb("lnc2", [128, D])
    onorm = sb("onorm", [128, VD])
    cwq = sb("cwq", [128, 4, 8])
    cwk = sb("cwk", [128, 4, 8])
    cwv = sb("cwv", [128, 4, 16])
    wa_s = sb("was", [128, 8, H1])
    wb_s = sb("wbs", [128, 8, H1])
    hbuf = sb("hbuf", [128, D])
    hT = sb("hT", [128, 8, CH])
    qraw = sb("qraw", [128, 8, CH + 3])
    kraw = sb("kraw", [128, 8, CH + 3])
    vraw = sb("vraw", [128, 16, CH + 3])
    qcv = sb("qcv", [128, 2 * CH])
    kcv = sb("kcv", [128, 2 * CH])
    vcv = sb("vcv", [128, 4 * CH])
    cvt = sb("cvt", [128, 4, CH])
    sqs = sb("sqs", [128, 2 * CH])
    rs2 = sb("rs2", [128, 4])
    ktm = sb("ktm", [128, KD])
    knfm = sb("knfm", [128, 2, CH])
    vtm = sb("vtm", [128, VD])
    a4 = sb("a4", [H1, CH])
    b4 = sb("b4", [H1, CH])
    sp4 = sb("sp4", [H1, CH])
    g4 = sb("g4", [H1, CH])
    c4 = sb("c4", [H1, CH])
    nc4 = sb("nc4", [H1, CH])
    wf4 = sb("wf4", [H1, CH])
    bet4 = sb("bet4", [H1, CH])
    cend4 = sb("cend4", [H1, 1])
    cendrow4 = sb("cendrow4", [1, H1])
    ctm4 = sb("ctm4", [128, H1])
    lam_tm = sb("lamtm", [128, H1])
    wtm1 = sb("wtm1", [128, H1])
    bettm = sb("bettm", [128, H1])
    lamend = sb("lamend", [128, H1])
    ED = sb("ED", [128, 2, CH])
    tmat = sb("tmat", [128, CH])
    Tt = sb("Tt", [128, CH])
    Mt = sb("Mt", [128, CH])
    Xb = sb("Xb", [128, CH])
    Xt = sb("Xt", [128, CH])
    Ab = sb("Ab", [128, CH])
    rhs_sb = sb("rhssb", [128, VD])
    bv = sb("bv", [128, VD])
    dl_tm = sb("dltm", [128, VD])
    kw = sb("kw", [128, KD])
    scrO = sb("scrO", [128, VD])
    cols = sb("cols", [128, 8])
    S0 = sb("S0", [128, 2, H1, VD])
    obuf = sb("obuf", [128, 2048])
    gts = sb("gts", [128, 512])
    ogT = sb("ogT", [128, 16, CH])
    xmid = sb("xmid", [128, D])
    h2T = sb("h2T", [128, 8, CH])
    scrA = sb("scrA", [128, 512])
    xsum = sb("xsum", [128, D])
    nst = sb("nst1", [128, 8])
    nsq = sb("nsq1", [128, D])
    pst = sb("pst1", [128, 8])
    ost = sb("ost1", [128, 8])
    ring = WRing(k, k.fresh("wrD"), 8, sb=sb)
    ld = k.ctr("ld0")

    with nc.Block() as b:
        @b.sync
        def _(sync):
            for dst, src in [(bo4[:], dr["bo4"]), (onorm[:], dr["onorm"]),
                             (cwq[:], dr["cwq"]), (cwk[:], dr["cwk"]), (cwv[:], dr["cwv"])]:
                sync.dma_start(out=dst, in_=src).then_inc(ld.sem, 16)
                ld.n += 16
            for dc in range(8):
                sync.dma_start(out=wa_s[:, dc, :], in_=dr["wa"][dc * 128:(dc + 1) * 128, :]).then_inc(ld.sem, 16)
                sync.dma_start(out=wb_s[:, dc, :], in_=dr["wb"][dc * 128:(dc + 1) * 128, :]).then_inc(ld.sem, 16)
                ld.n += 32
            sync.wait_ge(ld.sem, ld.n)
    with nc.Block() as b:
        @b.vector
        def _(ve):
            ve.memset(S0[:], 0.0)
            ve.memset(qraw[:, :, 0:3], 0.0)
            ve.memset(kraw[:, :, 0:3], 0.0)
            ve.memset(vraw[:, :, 0:3], 0.0)

    for c in range(NT):
        pipe = Pipe(k)
        pipe.rings = [ring]
        ring.begin()
        xt = x_res[:, c, :]

        j_lds1 = pipe.job(steps=[
            ("gpsimd", lambda s, b: s.dma_start(out=lnc0[:], in_=dr["ln1g"]), 16),
            ("gpsimd", lambda s, b: s.dma_start(out=lnc1[:], in_=dr["ln1b"]), 16),
            ("gpsimd", lambda s, b: s.dma_start(out=lnc2[:], in_=dr["rmsin"]), 16)])

        if c > 0:
            def f_halo(ve, banks):
                ve.tensor_copy(qraw[:, :, 0:3], qraw[:, :, CH:CH + 3])
                ve.tensor_copy(kraw[:, :, 0:3], kraw[:, :, CH:CH + 3])
                return ve.tensor_copy(vraw[:, :, 0:3], vraw[:, :, CH:CH + 3])
            pipe.job(steps=[("vector", f_halo)])

        j_ln = emit_ln(pipe, xt, hbuf[:], nst, nsq[:], g["epsc"][:, 1:2], gb=(lnc0[:], lnc1[:]),
                       after=[(j_lds1, 2)])
        j_rms = emit_rms(pipe, hbuf[:], hbuf[:], nst, nsq[:], g["epsc"][:, 2:3], 1.0 / D, wbc=lnc2[:],
                         after=[(j_ln, 3)])

        j_lds2 = pipe.job(steps=[
            ("gpsimd", lambda s, b: s.dma_start(out=lnc0[:], in_=dr["ln2g"]), 16),
            ("gpsimd", lambda s, b: s.dma_start(out=lnc1[:], in_=dr["ln2b"]), 16),
            ("gpsimd", lambda s, b: s.dma_start(out=lnc2[:], in_=dr["b2v"]), 16)],
            after=[(j_rms, 1)])

        jtr = []
        for dc in range(8):
            def f_pe(te, banks, dc=dc):
                return te.transpose(banks[0][:, 0:128], hbuf[:, dc * 128:(dc + 1) * 128], ident[:])

            def f_ev(ve, banks, dc=dc):
                return ve.tensor_copy(hT[:, dc, :], banks[0][:, 0:128])
            jtr.append(pipe.job(pe=f_pe, nb=1, steps=[("vector", f_ev)], after=[(j_rms, 1)]))
        j_hT = jtr[-1]

        def proj_jobs(wnm, nfc_total, rawbuf):
            jobs = []
            for grp in range((nfc_total + 3) // 4):
                fcs = list(range(grp * 4, min(grp * 4 + 4, nfc_total)))
                nfc = len(fcs)
                idxs = [ring.add(dr[wnm][dc * 128:(dc + 1) * 128,
                                         fcs[0] * 128: fcs[0] * 128 + nfc * 128], 128, nfc * 128)
                        for dc in range(8)]

                def f_pe(te, banks, idxs=idxs, nfc=nfc):
                    last = None
                    for dc in range(8):
                        ring.wait(te, idxs[dc])
                        tl = ring.tile(idxs[dc])
                        for i in range(nfc):
                            last = te.matmul(banks[i][:, 0:CH], tl[:, i * 128:(i + 1) * 128],
                                             hT[:, dc, :], start=(dc == 0), stop=(dc == 7))
                    return last

                def f_ev(ve, banks, fcs=fcs, nfc=nfc, rawbuf=rawbuf):
                    last = None
                    for i in range(nfc):
                        last = ve.tensor_copy(rawbuf[:, fcs[i], 3:3 + CH], banks[i][:, 0:CH])
                    return last
                jobs.append(pipe.job(pe=f_pe, nb=nfc, steps=[("vector", f_ev)], after=[(j_hT, 0)],
                                     wtiles=(ring, idxs)))
            return jobs

        jq = proj_jobs("wq", 8, qraw)
        jk = proj_jobs("wk", 8, kraw)
        jv = proj_jobs("wv", 16, vraw)
        j_proj = jv[-1]

        def f_ab(te, banks):
            last = None
            for dc in range(8):
                te.matmul(banks[0][0:H1, 0:CH], wa_s[:, dc, :], hT[:, dc, :],
                          start=(dc == 0), stop=(dc == 7))
                last = te.matmul(banks[1][0:H1, 0:CH], wb_s[:, dc, :], hT[:, dc, :],
                                 start=(dc == 0), stop=(dc == 7))
            return last

        def f_ab_ve(ve, banks):
            ve.tensor_copy(a4[:], banks[0][0:H1, 0:CH])
            return ve.tensor_copy(b4[:], banks[1][0:H1, 0:CH])
        j_ab = pipe.job(pe=f_ab, nb=2, steps=[("vector", f_ab_ve)], after=[(j_hT, 0)])

        def f_d1(sc, banks):
            sc.activation(sp4[:], a4[:], AF.Exp, bias=g["dtb1"][:])
            sc.drain()
            sc.activation(sp4[:], sp4[:], AF.Ln, bias=1.0)
            return sc.activation(bet4[:], b4[:], AF.Sigmoid)

        def f_d2(ve, banks):
            ve.tensor_scalar(g4[:], sp4[:], g["negea1"][:], None, ALU.mult)
            ve.tensor_tensor_scan(c4[:], g4[:], g["zeros"][0:H1, :], 0.0, ALU.add, ALU.add)
            ve.reduce_sum(cend4[:], g4[:], axis=AX.X)
            ve.tensor_scalar(nc4[:], c4[:], -1.0, None, ALU.mult)
            return ve.tensor_scalar(wf4[:], c4[:], cend4[:], None, ALU.subtract)

        def f_d3(sc, banks):
            return sc.activation(wf4[:], wf4[:], AF.Exp, scale=-1.0)
        j_d = pipe.job(steps=[("scalar", f_d1), ("vector", f_d2), ("scalar", f_d3)],
                       after=[(j_ab, 0)])

        def f_smt(te, banks):
            te.matmul(banks[0][0:128, 0:H1], c4[:], ident[0:H1, 0:H1],
                      start=True, stop=False, is_transpose=True)
            te.matmul(banks[0][0:1, 3 * H1:4 * H1], cend4[:], ident[0:H1, 0:H1],
                      start=False, stop=False, is_transpose=True)
            te.matmul(banks[0][0:128, H1:2 * H1], wf4[:], ident[0:H1, 0:H1],
                      start=False, stop=False, is_transpose=True)
            return te.matmul(banks[0][0:128, 2 * H1:3 * H1], bet4[:], ident[0:H1, 0:H1],
                             start=False, stop=True, is_transpose=True)

        def f_smt_ve(ve, banks):
            ve.tensor_copy(ctm4[:], banks[0][:, 0:H1])
            ve.tensor_copy(wtm1[:], banks[0][:, H1:2 * H1])
            ve.tensor_copy(bettm[:], banks[0][:, 2 * H1:3 * H1])
            return ve.tensor_copy(cendrow4[:], banks[0][0:1, 3 * H1:4 * H1])

        def f_smt_sc(sc, banks):
            return sc.activation(lam_tm[:], ctm4[:], AF.Exp)
        j_smt = pipe.job(pe=f_smt, nb=1, steps=[("vector", f_smt_ve), ("scalar", f_smt_sc)],
                         after=[(j_d, 2)])

        def f_de(te, banks):
            return te.matmul(banks[0][:, 0:H1], ones1[:], cendrow4[:], start=True, stop=True)

        def f_de_sc(sc, banks):
            return sc.activation(lamend[:], banks[0][:, 0:H1], AF.Exp)
        j_de = pipe.job(pe=f_de, nb=1, steps=[("scalar", f_de_sc)], after=[(j_smt, 0)])

        for h in range(H1):
            def f_cv(ve, banks, h=h):
                _conv_head(ve, qraw, cwq, 2 * h, 2, qcv, cvt)
                _conv_head(ve, kraw, cwk, 2 * h, 2, kcv, cvt)
                return _conv_head(ve, vraw, cwv, 4 * h, 4, vcv, cvt)

            def f_cv_sc(sc, banks):
                sc.activation(qcv[:], qcv[:], AF.Silu)
                sc.activation(kcv[:], kcv[:], AF.Silu)
                sc.activation(vcv[:], vcv[:], AF.Silu)
                sc.drain()
                return sc.activation(sqs[:], qcv[:], AF.Square)
            aft_cv = [(j_proj, 0)]
            if h > 0:
                aft_cv.append((j_S, "pe"))
            j_cv = pipe.job(steps=[("vector", f_cv), ("scalar", f_cv_sc)], after=aft_cv)

            def f_rs(te, banks):
                te.matmul(banks[0][:, 0:1], sqs[:, 0:CH], onescol[:], start=True, stop=False)
                return te.matmul(banks[0][:, 0:1], sqs[:, CH:2 * CH], onescol[:],
                                 start=False, stop=True)

            def f_rs_sc(sc, banks):
                return sc.activation(rs2[:, 0:1], banks[0][:, 0:1], AF.Sqrt, bias=g["epsc"][:, 2:3])

            def f_rs_ve(ve, banks):
                ve.reciprocal(rs2[:, 1:2], rs2[:, 0:1])
                return ve.tensor_scalar(rs2[:, 1:2], rs2[:, 1:2], 0.0625, None, ALU.mult)
            j_qrs = pipe.job(pe=f_rs, nb=1, steps=[("scalar", f_rs_sc), ("vector", f_rs_ve)],
                             after=[(j_cv, 1)])

            def f_ks_sc(sc, banks):
                return sc.activation(sqs[:], kcv[:], AF.Square)
            j_ksq = pipe.job(steps=[("scalar", f_ks_sc)], after=[(j_qrs, 0)])

            def f_krs(te, banks):
                te.matmul(banks[0][:, 0:1], sqs[:, 0:CH], onescol[:], start=True, stop=False)
                return te.matmul(banks[0][:, 0:1], sqs[:, CH:2 * CH], onescol[:],
                                 start=False, stop=True)

            def f_krs_sc(sc, banks):
                return sc.activation(rs2[:, 2:3], banks[0][:, 0:1], AF.Sqrt, bias=g["epsc"][:, 2:3])

            def f_krs_ve(ve, banks):
                return ve.reciprocal(rs2[:, 3:4], rs2[:, 2:3])
            j_krs = pipe.job(pe=f_krs, nb=1, steps=[("scalar", f_krs_sc), ("vector", f_krs_ve)],
                             after=[(j_ksq, 0)])

            def f_kt(te, banks):
                te.matmul(banks[0][:, 0:CH], kcv[:, 0:CH], ident[:],
                          start=True, stop=False, is_transpose=True)
                return te.matmul(banks[0][:, CH:2 * CH], kcv[:, CH:2 * CH], ident[:],
                                 start=False, stop=True, is_transpose=True)

            def f_kt_ve(ve, banks):
                return ve.tensor_scalar(ktm[:], banks[0][:, 0:KD], rs2[:, 3:4], None, ALU.mult)
            j_kt = pipe.job(pe=f_kt, nb=1, steps=[("vector", f_kt_ve)], after=[(j_krs, 1)])

            def f_kf(te, banks):
                te.matmul(banks[0][:, 0:CH], ktm[:, 0:CH], ident[:],
                          start=True, stop=False, is_transpose=True)
                return te.matmul(banks[0][:, CH:2 * CH], ktm[:, CH:2 * CH], ident[:],
                                 start=False, stop=True, is_transpose=True)

            def f_kf_ve(ve, banks):
                return ve.tensor_copy(knfm[:], banks[0][:, 0:KD])
            j_kf = pipe.job(pe=f_kf, nb=1, steps=[("vector", f_kf_ve)], after=[(j_kt, 0)])

            def f_vt(te, banks, h=h):
                last = None
                for i in range(4):
                    last = te.matmul(banks[0][:, i * CH:(i + 1) * CH], vcv[:, i * CH:(i + 1) * CH],
                                     ident[:], start=(i == 0), stop=(i == 3), is_transpose=True)
                return last

            def f_vt_ve(ve, banks):
                return ve.tensor_copy(vtm[:], banks[0][:, 0:VD])
            j_vt = pipe.job(pe=f_vt, nb=1, steps=[("vector", f_vt_ve)], after=[(j_cv, 1)])

            def f_D(te, banks, h=h):
                sel = bo4[:, h * CH:(h + 1) * CH]
                te.matmul(banks[0][:, 0:CH], c4[:], sel, start=True, stop=False)
                te.matmul(banks[0][:, 0:CH], sel, nc4[:], start=False, stop=True)
                te.matmul(banks[1][:, 0:CH], sel, c4[:], start=True, stop=False)
                return te.matmul(banks[1][:, 0:CH], nc4[:], sel, start=False, stop=True)

            def f_D_ve(ve, banks):
                ve.tensor_scalar(ED[:, 0, :], banks[0][:, 0:CH], 0.0, None, ALU.min)
                return ve.tensor_scalar(ED[:, 1, :], banks[1][:, 0:CH], 0.0, None, ALU.min)

            def f_D_sc(sc, banks):
                return sc.activation(ED[:], ED[:], AF.Exp)
            j_D = pipe.job(pe=f_D, nb=2, steps=[("vector", f_D_ve), ("scalar", f_D_sc)],
                           after=[(j_smt, 0)])

            def f_T(te, banks):
                te.matmul(banks[0][:, 0:CH], knfm[:, 0, :], knfm[:, 0, :], start=True, stop=False)
                return te.matmul(banks[0][:, 0:CH], knfm[:, 1, :], knfm[:, 1, :],
                                 start=False, stop=True)

            def f_T_ve(ve, banks, h=h):
                ve.tensor_tensor(tmat[:], banks[0][:, 0:CH], m_lo_s[:], ALU.mult)
                return ve.scalar_tensor_tensor(tmat[:], ED[:, 0, :], bettm[:, h:h + 1],
                                               tmat[:], ALU.mult, ALU.mult)
            j_T = pipe.job(pe=f_T, nb=1, steps=[("vector", f_T_ve)],
                           after=[(j_kf, 0), (j_D, 1)])

            def f_i0(te, banks):
                return te.transpose(banks[0][:, 0:CH], tmat[:], ident[:])

            def f_i0_ve(ve, banks):
                ve.tensor_copy(Tt[:], banks[0][:, 0:CH])
                return ve.tensor_tensor(Mt[:], ident[:], banks[0][:, 0:CH], ALU.subtract)
            j_i = pipe.job(pe=f_i0, nb=1, steps=[("vector", f_i0_ve)], after=[(j_T, 0)])

            def f_x1(te, banks):
                return te.matmul(banks[0][:, 0:CH], Tt[:], tmat[:], start=True, stop=True)

            def f_x1_ve(ve, banks):
                return ve.tensor_copy(Xb[:], banks[0][:, 0:CH])
            j_x = pipe.job(pe=f_x1, nb=1, steps=[("vector", f_x1_ve)], after=[(j_i, 0)])

            for kk in range(1, 7):
                def f_m(te, banks):
                    return te.matmul(banks[0][:, 0:CH], Xb[:], Mt[:], start=True, stop=True)

                def f_m_ve(ve, banks):
                    return ve.tensor_tensor(Mt[:], Mt[:], banks[0][:, 0:CH], ALU.add)
                j_m = pipe.job(pe=f_m, nb=1, steps=[("vector", f_m_ve)], after=[(j_x, 0)])
                if kk < 6:
                    def f_t(te, banks):
                        return te.transpose(banks[0][:, 0:CH], Xb[:], ident[:])

                    def f_t_ve(ve, banks):
                        return ve.tensor_copy(Xt[:], banks[0][:, 0:CH])
                    j_t = pipe.job(pe=f_t, nb=1, steps=[("vector", f_t_ve)], after=[(j_m, "pe")])

                    def f_s(te, banks):
                        return te.matmul(banks[0][:, 0:CH], Xt[:], Xb[:], start=True, stop=True)

                    def f_s_ve(ve, banks):
                        return ve.tensor_copy(Xb[:], banks[0][:, 0:CH])
                    j_x = pipe.job(pe=f_s, nb=1, steps=[("vector", f_s_ve)], after=[(j_t, 0)])

            j_inv = j_m

            def f_rhs(te, banks, h=h):
                te.matmul(banks[0][:, 0:VD], knfm[:, 0, :], S0[:, 0, h, :], start=True, stop=False)
                return te.matmul(banks[0][:, 0:VD], knfm[:, 1, :], S0[:, 1, h, :],
                                 start=False, stop=True)

            def f_rhs_ve(ve, banks, h=h):
                ve.tensor_scalar(bv[:], vtm[:], bettm[:, h:h + 1], None, ALU.mult)
                ve.tensor_tensor(cols[:, 0:1], lam_tm[:, h:h + 1], bettm[:, h:h + 1], ALU.mult)
                ve.tensor_scalar(cols[:, 1:2], cols[:, 0:1], -1.0, None, ALU.mult)
                return ve.scalar_tensor_tensor(rhs_sb[:], banks[0][:, 0:VD], cols[:, 1:2],
                                               bv[:], ALU.mult, ALU.add)
            j_rhs = pipe.job(pe=f_rhs, nb=1, steps=[("vector", f_rhs_ve)],
                             after=[(j_kf, 0), (j_vt, 0), (j_smt, 1)])

            def f_dl(te, banks):
                return te.matmul(banks[0][:, 0:VD], Mt[:], rhs_sb[:], start=True, stop=True)

            def f_dl_ve(ve, banks, h=h):
                ve.tensor_copy(dl_tm[:], banks[0][:, 0:VD])
                return ve.tensor_scalar(kw[:], ktm[:], wtm1[:, h:h + 1], None, ALU.mult)
            j_dl = pipe.job(pe=f_dl, nb=1, steps=[("vector", f_dl_ve)],
                            after=[(j_inv, 0), (j_rhs, 0)])

            def f_A(te, banks):
                te.matmul(banks[0][:, 0:CH], knfm[:, 0, :], qcv[:, 0:CH], start=True, stop=False)
                return te.matmul(banks[0][:, 0:CH], knfm[:, 1, :], qcv[:, CH:2 * CH],
                                 start=False, stop=True)

            def f_A_ve(ve, banks):
                ve.tensor_tensor(Ab[:], banks[0][:, 0:CH], m_up[:], ALU.mult)
                return ve.tensor_tensor(Ab[:], Ab[:], ED[:, 1, :], ALU.mult)
            j_A = pipe.job(pe=f_A, nb=1, steps=[("vector", f_A_ve)],
                           after=[(j_kf, 0), (j_D, 1)])

            def f_O(te, banks, h=h):
                te.matmul(banks[0][:, 0:VD], qcv[:, 0:CH], S0[:, 0, h, :], start=True, stop=False)
                te.matmul(banks[0][:, 0:VD], qcv[:, CH:2 * CH], S0[:, 1, h, :],
                          start=False, stop=True)
                return te.matmul(banks[1][:, 0:VD], Ab[:], dl_tm[:], start=True, stop=True)

            def f_O_ve(ve, banks, h=h):
                ve.tensor_tensor(cols[:, 2:3], lam_tm[:, h:h + 1], rs2[:, 1:2], ALU.mult)
                ve.tensor_scalar(scrO[:], banks[0][:, 0:VD], cols[:, 2:3], None, ALU.mult)
                ve.tensor_scalar(bv[:], banks[1][:, 0:VD], rs2[:, 1:2], None, ALU.mult)
                return ve.tensor_tensor(obuf[:, h * VD:(h + 1) * VD], scrO[:], bv[:], ALU.add)
            j_O = pipe.job(pe=f_O, nb=2, steps=[("vector", f_O_ve)],
                           after=[(j_dl, 0), (j_A, 0), (j_qrs, 1)])

            def f_S(te, banks):
                te.matmul(banks[0][:, 0:VD], kw[:, 0:CH], dl_tm[:], start=True, stop=True)
                return te.matmul(banks[1][:, 0:VD], kw[:, CH:2 * CH], dl_tm[:],
                                 start=True, stop=True)

            def f_S_ve(ve, banks, h=h):
                ve.scalar_tensor_tensor(S0[:, 0, h, :], S0[:, 0, h, :], lamend[:, h:h + 1],
                                        banks[0][:, 0:VD], ALU.mult, ALU.add)
                return ve.scalar_tensor_tensor(S0[:, 1, h, :], S0[:, 1, h, :], lamend[:, h:h + 1],
                                               banks[1][:, 0:VD], ALU.mult, ALU.add)
            j_S = pipe.job(pe=f_S, nb=2, steps=[("vector", f_S_ve)],
                           after=[(j_dl, 0), (j_de, 0), (j_O, "pe")])

        j_gt_list = []
        for fg in range(4):
            j_on = emit_rms(pipe, obuf[:, fg * VD:(fg + 1) * VD], obuf[:, fg * VD:(fg + 1) * VD],
                            ost, nsq[:, 0:VD], g["epsc"][:, 2:3], 1.0 / VD, wbc=onorm[:], after=[(j_S, 0)])

            idxs = [ring.add(dr["wg1"][dc * 128:(dc + 1) * 128, fg * 512:(fg + 1) * 512], 128, 512)
                    for dc in range(8)]

            def f_gt(te, banks, idxs=idxs):
                last = None
                for dc in range(8):
                    ring.wait(te, idxs[dc])
                    last = te.matmul(banks[0][:, 0:512], hT[:, dc, :], ring.tile(idxs[dc]),
                                     start=(dc == 0), stop=(dc == 7))
                return last

            def f_gt_sc(sc, banks):
                return sc.activation(gts[:], banks[0][:, 0:512], AF.Silu)

            def f_gt_ve(ve, banks, fg=fg):
                return ve.tensor_tensor(obuf[:, fg * 512:(fg + 1) * 512],
                                        obuf[:, fg * 512:(fg + 1) * 512], gts[:], ALU.mult)
            aft = [(j_on, 1)]
            if j_gt_list:
                aft.append((j_gt_list[-1], 1))
            j_gt_list.append(pipe.job(pe=f_gt, nb=1,
                                      steps=[("scalar", f_gt_sc), ("vector", f_gt_ve)],
                                      after=aft, wtiles=(ring, idxs)))
        j_gt = j_gt_list[-1]

        jyt = []
        for ec in range(16):
            def f_pe(te, banks, ec=ec):
                return te.transpose(banks[0][:, 0:128], obuf[:, ec * 128:(ec + 1) * 128], ident[:])

            def f_ev(ve, banks, ec=ec):
                return ve.tensor_copy(ogT[:, ec, :], banks[0][:, 0:128])
            jyt.append(pipe.job(pe=f_pe, nb=1, steps=[("vector", f_ev)], after=[(j_gt, 1)]))

        jwo = []
        for fo in range(2):
            idxs = [ring.add(dr["wo"][ec * 128:(ec + 1) * 128, fo * 512:(fo + 1) * 512], 128, 512)
                    for ec in range(16)]

            def f_pe1(te, banks, idxs=idxs):
                last = None
                for j in range(8):
                    ring.wait(te, idxs[j])
                    last = te.matmul(banks[0][:, 0:512], ogT[:, j, :], ring.tile(idxs[j]),
                                     start=(j == 0), stop=False)
                return last

            def f_pe2(te, banks, idxs=idxs):
                last = None
                for j in range(8, 16):
                    ring.wait(te, idxs[j])
                    last = te.matmul(banks[0][:, 0:512], ogT[:, j, :], ring.tile(idxs[j]),
                                     start=False, stop=(j == 15))
                return last

            def f_ev(ve, banks, fo=fo):
                return ve.scalar_tensor_tensor(xmid[:, fo * 512:(fo + 1) * 512],
                                               xt[:, fo * 512:(fo + 1) * 512],
                                               1.0, banks[0][:, 0:512], ALU.mult, ALU.add)
            ja = pipe.job(pe=f_pe1, nb=1, after=[(jyt[-1], 0)], wtiles=(ring, idxs[:8]))
            jwo.append(pipe.job(pe=f_pe2, banks_from=ja, steps=[("vector", f_ev)],
                                wtiles=(ring, idxs[8:])))

        j_ln2 = emit_ln(pipe, xmid[:], hbuf[:], nst, nsq[:], g["epsc"][:, 1:2], gb=(lnc0[:], lnc1[:]),
                        after=[(jwo[-1], 0), (j_lds2, 2)])

        jh2 = []
        for dc in range(8):
            def f_pe(te, banks, dc=dc):
                return te.transpose(banks[0][:, 0:128], hbuf[:, dc * 128:(dc + 1) * 128], ident[:])

            def f_ev(ve, banks, dc=dc):
                return ve.tensor_copy(h2T[:, dc, :], banks[0][:, 0:128])
            jh2.append(pipe.job(pe=f_pe, nb=1, steps=[("vector", f_ev)], after=[(j_ln2, 3)]))

        jw2 = []
        for fo in range(2):
            idxs = [ring.add(dr["w2m"][dc * 128:(dc + 1) * 128, fo * 512:(fo + 1) * 512], 128, 512)
                    for dc in range(8)]

            def f_pe(te, banks, idxs=idxs):
                last = None
                for j in range(8):
                    ring.wait(te, idxs[j])
                    last = te.matmul(banks[0][:, 0:512], h2T[:, j, :], ring.tile(idxs[j]),
                                     start=(j == 0), stop=(j == 7))
                return last

            def f_e1(ve, banks, fo=fo):
                return ve.tensor_tensor(scrA[:], banks[0][:, 0:512],
                                        lnc2[:, fo * 512:(fo + 1) * 512], ALU.add)

            def f_e2(sc, banks):
                return sc.activation(scrA[:], scrA[:], AF.Gelu)

            def f_e3(ve, banks, fo=fo):
                ve.tensor_tensor(scrA[:], scrA[:], xmid[:, fo * 512:(fo + 1) * 512], ALU.add)
                return ve.tensor_tensor(xsum[:, fo * 512:(fo + 1) * 512], scrA[:],
                                        xt[:, fo * 512:(fo + 1) * 512], ALU.add)
            aft_w2 = [(jh2[-1], 0)] + ([(jw2[0], 2)] if jw2 else [])
            jw2.append(pipe.job(pe=f_pe, nb=1,
                                steps=[("vector", f_e1), ("scalar", f_e2), ("vector", f_e3)],
                                after=aft_w2, wtiles=(ring, idxs)))

        emit_ln(pipe, xsum[:], xt, pst, nsq[:], g["epsc"][:, 1:2], after=[(jw2[-1], 2)])

        with nc.Block() as blk:
            pipe.run(blk)
    es2.close()


def build(blocks=BLOCKS):
    nc = bass.Bass("TRN2", target_bir_lowering=False)
    es = contextlib.ExitStack()
    k = K(nc, es)
    dr = {"x": nc.dram_tensor("x", [SEQ * T, D], F32, kind="ExternalInput").ap()}
    for nm, shp in WSPECS.items():
        dr[nm] = nc.dram_tensor(nm, list(shp), F32, kind="ExternalInput").ap()
    y_out = nc.dram_tensor("y", [SEQ * T, D], F32, kind="ExternalOutput").ap()

    x_res = k.sb("x_res", [128, NT, D])
    xg2_dram = nc.dram_tensor("xg2_scr", [NDB * 128, D], F32, kind="Internal").ap()
    eo_dram = nc.dram_tensor("eo_scr", [NDB * 128, D], F32, kind="Internal").ap()
    ident = k.sb("ident_s", [128, 128])
    ones1 = k.sb("ones1_s", [1, 128])
    onescol = k.sb("onescol_s", [128, 1])
    m_up = k.sb("m_up_s", [128, 128])
    m_up_st = k.sb("m_up_st_s", [128, 128])
    iotabc = k.sb("iotabc_s", [128, 128])
    iotacol = k.sb("iotacol_s", [128, 1])
    m_lo_i = k.sb("m_lo_i_s", [128, 128])
    m_lo_s = k.sb("m_lo_s_s", [128, 128])
    negea0 = k.sb("negea0", [NH, 1])
    negea1 = k.sb("negea1", [H1, 1])
    dtb0 = k.sb("dtb0_s", [NH, 1])
    dtb1 = k.sb("dtb1_s", [H1, 1])
    d0bc = k.sb("d0bc", [128, NH])
    gw0 = k.sb("gw0_s", [128, 8, 8])
    gw1 = k.sb("gw1_s", [128, 8, 8])
    zeros = k.sb("zeros_s", [128, 128])
    epsc = k.sb("epsc_s", [128, 4])
    drow = k.sb("drow", [1, NH])

    ld = k.ctr("ld0")
    st = k.ctr("st0")

    with nc.Block() as b:
        @b.sync
        def _(sync):
            for dst, src in [(ident[:], dr["ident"]), (ones1[:], dr["ones128"]),
                             (onescol[:], dr["onescol"]),
                             (m_up[:], dr["m_up_incl"]), (m_lo_i[:], dr["m_lo_incl"]),
                             (m_up_st[:], dr["m_up_strict"]), (iotabc[:], dr["iotabc"]),
                             (iotacol[:], dr["iotacol"]),
                             (m_lo_s[:], dr["m_lo_strict"]),
                             (dtb0[:], dr["dtb0"]), (dtb1[:], dr["dtb1"]),
                             (negea0[:], dr["alog0"]), (negea1[:], dr["alog1"]),
                             (drow[:], dr["d0row"]), (epsc[:], dr["epsc"])]:
                sync.dma_start(out=dst, in_=src).then_inc(ld.sem, 16)
                ld.n += 16
            for dc in range(8):
                sync.dma_start(out=gw0[:, dc, :], in_=dr["gw0"][dc * 128:(dc + 1) * 128, :]).then_inc(ld.sem, 16)
                sync.dma_start(out=gw1[:, dc, :], in_=dr["gw1"][dc * 128:(dc + 1) * 128, :]).then_inc(ld.sem, 16)
                ld.n += 32
            sync.wait_ge(ld.sem, ld.n)
    with nc.Block() as b:
        @b.scalar
        def _(sc):
            sc.activation(negea0[:], negea0[:], AF.Exp)
            sc.activation(negea1[:], negea1[:], AF.Exp)
        @b.vector
        def _(ve):
            ve.memset(zeros[:], 0.0)
    with nc.Block() as b:
        @b.vector
        def _(ve):
            ve.tensor_scalar(negea0[:], negea0[:], -1.0, None, ALU.mult)
            ve.tensor_scalar(negea1[:], negea1[:], -1.0, None, ALU.mult)
    with nc.Block() as b:
        @b.tensor
        def _(te):
            te.matmul(k.banks.t[0][:, 0:NH], ones1[:], drow[:], start=True, stop=True)
    with nc.Block() as b:
        @b.vector
        def _(ve):
            ve.tensor_copy(d0bc[:], k.banks.t[0][:, 0:NH])

    g = dict(ident=ident[:], ones1=ones1[:], onescol=onescol[:], ones11=ones1[0:1, 0:1],
             m_up=m_up[:], m_up_st=m_up_st[:], iotabc=iotabc[:], iotacol=iotacol[:],
             m_lo_i=m_lo_i[:], m_lo_s=m_lo_s[:], negea0=negea0[:], negea1=negea1[:],
             dtb0=dtb0[:], dtb1=dtb1[:], d0bc=d0bc[:], gw0=gw0, gw1=gw1, zeros=zeros[:], epsc=epsc)

    if MOE_SPARSE and ("l0f" in blocks or "l1f" in blocks):
        with contextlib.ExitStack() as esz:
            zb1k = esz.enter_context(nc.sbuf_tensor("zinit", [128, D], F32))
            with nc.Block() as b:
                @b.vector
                def _(ve):
                    ve.memset(zb1k[:], 0.0)
            with nc.Block() as b:
                @b.sync
                def _(sync):
                    for i in range(NDB):
                        sync.dma_start(out=xg2_dram[i * 128:(i + 1) * 128, :], in_=zb1k[:]).then_inc(ld.sem, 16)
                        sync.dma_start(out=eo_dram[i * 128:(i + 1) * 128, :], in_=zb1k[:]).then_inc(ld.sem, 16)
                        ld.n += 32
                    sync.wait_ge(ld.sem, ld.n)

    for seq in range(SEQ):
        with nc.Block() as b:
            @b.sync
            def _(sync):
                sync.wait_ge(st.sem, st.n)
                for t in range(NT):
                    sync.dma_start(out=x_res[:, t, :],
                                   in_=dr["x"][seq * T + t * 128: seq * T + (t + 1) * 128, :]).then_inc(ld.sem, 16)
                    ld.n += 16
                sync.wait_ge(ld.sem, ld.n)
        if "l0a" in blocks:
            _l0_mamba(k, dr, x_res, seq, g)
        if "l0f" in blocks:
            if MOE_SPARSE:
                _moe_sparse(k, dr, x_res, seq, 0, g, (xg2_dram, eo_dram))
            else:
                _moe(k, dr, x_res, seq, 0, g)
        if "l1a" in blocks:
            _l1_delta(k, dr, x_res, seq, g)
        if "l1f" in blocks:
            if MOE_SPARSE:
                _moe_sparse(k, dr, x_res, seq, 1, g, (xg2_dram, eo_dram))
            else:
                _moe(k, dr, x_res, seq, 1, g)
        with nc.Block() as b:
            @b.sync
            def _(sync):
                for t in range(NT):
                    sync.dma_start(out=y_out[seq * T + t * 128: seq * T + (t + 1) * 128, :],
                                   in_=x_res[:, t, :]).then_inc(st.sem, 16)
                    st.n += 16

    with nc.Block() as b:
        @b.sync
        def _(sync):
            sync.wait_ge(st.sem, st.n)
    es.close()
    return nc


def kernel(**inputs):
    a = _prep(inputs)
    x = np.asarray(inputs["L0_attn_x"], dtype=np.float32)
    nc = build(BLOCKS)
    in_maps = []
    for cc in range(NCORE):
        m = dict(a)
        m["x"] = np.ascontiguousarray(x[2 * cc:2 * cc + 2].reshape(SEQ * T, D))
        in_maps.append(m)
    res = run_bass_kernel_spmd(nc, in_maps, core_ids=list(range(NCORE)))
    out = np.empty((16, T, D), np.float32)
    for cc in range(NCORE):
        out[2 * cc:2 * cc + 2] = res.results[cc]["y"].reshape(SEQ, T, D)
    return out


NTG = 24
NDB = 2 * NTG


class WRingD:

    def __init__(self, k: K, name, nslots, estride, tgexp_row, sb):
        self.k = k
        self.nslots = nslots
        self.estride = estride
        self.tgexp = tgexp_row
        self.buf = sb(name, [128, nslots, 512])
        self.slot_ctr = [k.ctr(f"{name}_s{s}") for s in range(nslots)]
        self.loads = []
        self.base = [c.n for c in self.slot_ctr]
        self.freed = {}

    def begin(self):
        self.loads = []
        self.base = [c.n for c in self.slot_ctr]
        self.freed = {}

    def add(self, handle, off, rowstride, rows, cols, tg):
        self.loads.append((handle, off, rowstride, rows, cols, tg))
        return len(self.loads) - 1

    def tile(self, i):
        _, _, _, rows, cols, _ = self.loads[i]
        return self.buf[:, i % self.nslots, :][0:rows, 0:cols]

    def wait(self, te, i):
        s = i % self.nslots
        te.wait_ge(self.slot_ctr[s].sem, self.base[s] + 16 * (i // self.nslots + 1))

    def emit_loader(self, block):
        if not self.loads:
            return
        loads, ns = self.loads, self.nslots
        buf, freed, sc, estride, tgexp = self.buf, self.freed, self.slot_ctr, self.estride, self.tgexp
        kk = self.k

        @block.gpsimd
        def _(gpe):
            with gpe.register(kk.fresh("rwe")) as re, \
                 gpe.register(kk.fresh("rwb")) as rb, \
                 gpe.register(kk.fresh("rw2")) as r2:
                cur_tg = None
                for i, (handle, off, rowstride, rows, cols, tg) in enumerate(loads):
                    if tg != cur_tg:
                        gpe.reg_load(re, tgexp[0:1, tg:tg + 1])
                        gpe.reg_mul(rb, re, estride)
                        cur_tg = tg
                    if i >= ns:
                        c, v = freed[i - ns]
                        gpe.wait_ge(c.sem, v)
                    gpe.reg_add(r2, rb, off)
                    dst = buf[:, i % ns, :][0:rows, 0:cols]
                    gpe.dma_start(out=dst,
                                  in_=bass.AP(handle, r2, [[rowstride, 128], [1, cols]])
                                  ).then_inc(sc[i % ns].sem, 16)
        for i in range(len(loads)):
            sc[i % ns].n += 16


def _moe_sparse(k: K, dr, x_res, seq, layer, g, scratch):
    nc = k.nc
    es2 = contextlib.ExitStack()
    I32 = mybir.dt.int32
    sb = lambda n, s, dt=F32: es2.enter_context(nc.sbuf_tensor(k.fresh(n), list(s), dt))
    ident = g["ident"]
    gw = g["gw0"] if layer == 0 else g["gw1"]
    DE = DE0 if layer == 0 else DE1
    NFC = DE // 128
    NGRP = (NFC + 3) // 4
    xg2_dram, eo_dram = scratch
    ld = k.ctr("ld0")

    wsel = sb("wsel", [128, 16, 2])
    dsel_i = sb("dseli", [128, 16, 2], I32)
    tgexp_i = sb("tgexpi", [1, 128], I32)
    tgexp_f = sb("tgexpf", [1, 128])
    rmsw = sb("rmsw", [128, D]) if layer == 1 else None
    ybuf = sb("ybuf", [128, D]) if layer == 1 else None
    nsq = sb("nsq", [128, D])
    xsum = sb("xsum", [128, D])

    if layer == 1:
        with nc.Block() as b:
            @b.sync
            def _(sync):
                sync.dma_start(out=rmsw[:], in_=dr["rmsw1"]).then_inc(ld.sem, 16)
                ld.n += 16
                sync.wait_ge(ld.sem, ld.n)

    es3 = contextlib.ExitStack()
    sb3 = lambda n, s, dt=F32: es3.enter_context(nc.sbuf_tensor(k.fresh(n), list(s), dt))
    mb1 = sb3("mb1", [128, 16, 8])
    mb2 = sb3("mb2", [128, 16, 8])
    msel = sb3("msel", [128, 16, 8])
    rkb = sb3("rkb", [128, 16, 8])
    runrow = sb3("runrow", [1, 8])
    rows8 = sb3("rows8", [1, 40])
    xTb = sb3("xTb", [128, 8, 128])
    rsts = [sb3("rst", [128, 8]) for _ in range(2)]
    scrs = [sb3("scr", [128, 8]) for _ in range(2)]
    shcol = sb3("shcol", [128, 2])
    tgef = sb3("tgef", [128, 8])
    dself = sb3("dself", [128, 16, 2])

    with nc.Block() as blk0:
        @blk0.vector
        def _(ve):
            ve.memset(runrow[:], 0.0)

    pipe = Pipe(k)
    jprev_rt = None
    jrk_prev = None
    jrt_all = []
    for tb in range(16):
        xt = x_res[:, tb, :]
        if layer == 1:
            aft_n = ([(jprev_rt, 2), (jprev_rt, "pe")] if jprev_rt is not None else [])
            jn = emit_rms(pipe, xt, ybuf[:], rsts[tb % 2], nsq[:], g["epsc"][:, 2:3], 1.0 / D,
                          wbc=rmsw[:], after=aft_n)
            src_t = ybuf[:]
            aft_tr = [(jn, 1)]
        else:
            src_t = xt
            aft_tr = ([(jprev_rt, "pe")] if jprev_rt is not None else [])

        jtr = None
        for dc in range(8):
            def f_pe(te, banks, dc=dc, src_t=src_t):
                return te.transpose(banks[0][:, 0:128], src_t[:, dc * 128:(dc + 1) * 128], ident[:])

            def f_ev(ve, banks, dc=dc):
                return ve.tensor_copy(xTb[:, dc, :], banks[0][:, 0:128])
            jtr = pipe.job(pe=f_pe, nb=1, steps=[("vector", f_ev)], after=aft_tr)

        def f_lg(te, banks, tb=tb):
            last = None
            for dc in range(8):
                last = te.matmul(banks[0][:, 0:8], xTb[:, dc, :], gw[:, dc, :],
                                 start=(dc == 0), stop=(dc == 7))
            return last

        rst, scr = rsts[tb % 2], scrs[tb % 2]

        def f_r1(ve, banks):
            ve.reduce_max(rst[:, 0:1], banks[0][:, 0:8], axis=AX.X)
            return ve.tensor_scalar(rst[:, 1:2], rst[:, 0:1], -1.0, None, ALU.mult)

        def f_r2(sc, banks):
            return sc.activation(scr[:], banks[0][:, 0:8], AF.Exp, bias=rst[:, 1:2],
                                 accum_out=rst[:, 2:3])

        def f_r3(ve, banks, tb=tb):
            s = scr[:]
            if layer == 0:
                ve.reciprocal(rst[:, 3:4], rst[:, 2:3])
                ve.tensor_scalar(s, s, rst[:, 3:4], None, ALU.mult)
            ve.reduce_max(rst[:, 4:5], s, axis=AX.X)
            ve.tensor_scalar(mb1[:, tb, :], s, rst[:, 4:5], None, ALU.is_ge)
            ve.scalar_tensor_tensor(mb2[:, tb, :], mb1[:, tb, :], -1e30, s, ALU.mult, ALU.add)
            ve.reduce_max(rst[:, 5:6], mb2[:, tb, :], axis=AX.X)
            ve.tensor_scalar(mb2[:, tb, :], mb2[:, tb, :], rst[:, 5:6], None, ALU.is_ge)
            ve.tensor_tensor(rst[:, 6:7], rst[:, 4:5], rst[:, 5:6], ALU.add)
            if layer == 0:
                ve.tensor_scalar(rst[:, 6:7], rst[:, 6:7], 1e-20, None, ALU.add)
            ve.reciprocal(rst[:, 7:8], rst[:, 6:7])
            ve.tensor_tensor(wsel[:, tb, 0:1], rst[:, 4:5], rst[:, 7:8], ALU.mult)
            ve.tensor_tensor(wsel[:, tb, 1:2], rst[:, 5:6], rst[:, 7:8], ALU.mult)
            return ve.tensor_tensor(msel[:, tb, :], mb1[:, tb, :], mb2[:, tb, :], ALU.add)
        j_rt = pipe.job(pe=f_lg, nb=1, steps=[("vector", f_r1), ("scalar", f_r2),
                                              ("vector", f_r3)], after=[(jtr, 0)])
        jprev_rt = j_rt
        jrt_all.append(j_rt)

        def f_rk(te, banks, tb=tb):
            te.matmul(banks[0][:, 0:8], g["m_up_st"], msel[:, tb, :], start=True, stop=False)
            te.matmul(banks[0][:, 0:8], g["ones1"], runrow[:], start=False, stop=True)
            te.matmul(banks[1][0:1, 0:8], g["onescol"], msel[:, tb, :], start=True, stop=False)
            return te.matmul(banks[1][0:1, 0:8], g["ones11"], runrow[:], start=False, stop=True)

        def f_rk_ve(ve, banks, tb=tb):
            ve.tensor_copy(rkb[:, tb, :], banks[0][:, 0:8])
            return ve.tensor_copy(runrow[:], banks[1][0:1, 0:8])
        aft_rk = [(j_rt, 2)]
        if jrk_prev is not None:
            aft_rk.append((jrk_prev, 0))
        jrk_prev = pipe.job(pe=f_rk, nb=2, steps=[("vector", f_rk_ve)], after=aft_rk)

    def f_fin(ve, banks):
        ve.tensor_copy(rows8[0:1, 0:8], runrow[:])
        ve.tensor_scalar(rows8[0:1, 8:16], rows8[0:1, 0:8], 0.0, None, ALU.is_gt)
        for kk in range(1, 8):
            ve.tensor_scalar(scrs[0][0:1, 0:8], rows8[0:1, 0:8], float(256 * kk), None, ALU.is_gt)
            ve.tensor_tensor(rows8[0:1, 8:16], rows8[0:1, 8:16], scrs[0][0:1, 0:8], ALU.add)
        ve.tensor_scalar(rows8[0:1, 8:16], rows8[0:1, 8:16], 256.0, None, ALU.mult)
        ve.tensor_tensor_scan(rows8[0:1, 16:24], rows8[0:1, 8:16], g["zeros"][0:1, 0:8],
                              0.0, ALU.add, ALU.add)
        ve.tensor_tensor(rows8[0:1, 24:32], rows8[0:1, 16:24], rows8[0:1, 8:16], ALU.subtract)
        return ve.tensor_scalar(rows8[0:1, 32:40], rows8[0:1, 16:24], 1.0 / 256.0, None, ALU.mult)
    j_fin = pipe.job(steps=[("vector", f_fin)], after=[(jrk_prev, 0)])

    def f_tg(te, banks):
        return te.matmul(banks[0][:, 0:8], g["ones1"], rows8[0:1, 32:40], start=True, stop=True)

    def f_tg_ve(ve, banks):
        ve.tensor_scalar(tgef[:], banks[0][:, 0:8], g["iotacol"], None, ALU.is_le)
        ve.reduce_sum(shcol[:, 0:1], tgef[:], axis=AX.X)
        return ve.tensor_scalar(shcol[:, 0:1], shcol[:, 0:1], 7.0, None, ALU.min)
    j_tg = pipe.job(pe=f_tg, nb=1, steps=[("vector", f_tg_ve)], after=[(j_fin, 0)])

    def f_tg2(te, banks):
        return te.matmul(banks[0][0:1, 0:128], shcol[:, 0:1], ident[:],
                         start=True, stop=True, is_transpose=True)

    def f_tg2_ve(ve, banks):
        ve.tensor_copy(tgexp_f[:], banks[0][0:1, 0:128])
        return ve.tensor_copy(tgexp_i[:], banks[0][0:1, 0:128])
    pipe.job(pe=f_tg2, nb=1, steps=[("vector", f_tg2_ve)], after=[(j_tg, 0)])

    jds_prev = None
    for tb in range(16):
        def f_ds(te, banks):
            return te.matmul(banks[0][:, 0:8], g["ones1"], rows8[0:1, 24:32],
                             start=True, stop=True)

        def f_ds_ve(ve, banks, tb=tb):
            ve.tensor_tensor(tgef[:], rkb[:, tb, :], banks[0][:, 0:8], ALU.add)
            ve.tensor_tensor(scrs[0][:], mb1[:, tb, :], tgef[:], ALU.mult)
            ve.reduce_sum(dself[:, tb, 0:1], scrs[0][:], axis=AX.X)
            ve.tensor_tensor(scrs[0][:], mb2[:, tb, :], tgef[:], ALU.mult)
            ve.reduce_sum(dself[:, tb, 1:2], scrs[0][:], axis=AX.X)
            ve.tensor_scalar(dself[:, tb, 0:1], dself[:, tb, 0:1], float(NDB * 128 - 1), None, ALU.min)
            ve.tensor_scalar(dself[:, tb, 1:2], dself[:, tb, 1:2], float(NDB * 128 - 1), None, ALU.min)
            return ve.tensor_copy(dsel_i[:, tb, :], dself[:, tb, :])
        aft_ds = [(j_fin, 0)]
        if jds_prev is not None:
            aft_ds.append((jds_prev, 0))
        j_ds = pipe.job(pe=f_ds, nb=1, steps=[("vector", f_ds_ve)], after=aft_ds)

        if layer == 1:
            jn2 = emit_rms(pipe, x_res[:, tb, :], ybuf[:], rsts[tb % 2], nsq[:],
                           g["epsc"][:, 2:3], 1.0 / D, wbc=rmsw[:],
                           after=([(jds_prev, 1)] if jds_prev is not None else []))
            src_s = ybuf[:]
            aft_sc = [(j_ds, 0), (jn2, 1)]
        else:
            src_s = x_res[:, tb, :]
            aft_sc = [(j_ds, 0)]

        def f_sc1(gpe, banks, tb=tb, src_s=src_s):
            return gpe.indirect_dma_start(
                out=xg2_dram, out_offset=bass.IndirectOffsetOnAxis(ap=dsel_i[:, tb, 0:1], axis=0),
                in_=src_s, in_offset=None)

        def f_sc2(gpe, banks, tb=tb, src_s=src_s):
            return gpe.indirect_dma_start(
                out=xg2_dram, out_offset=bass.IndirectOffsetOnAxis(ap=dsel_i[:, tb, 1:2], axis=0),
                in_=src_s, in_offset=None)
        jds_prev = pipe.job(steps=[("gpsimd", f_sc1, 16), ("gpsimd", f_sc2, 16)], after=aft_sc)

    with nc.Block() as blk:
        pipe.run(blk)
    es3.close()
    gp_ctr = k.ctr("st_gpsimd")
    scat_done = ("ctr", gp_ctr, gp_ctr.n)

    es35 = contextlib.ExitStack()
    sb35 = lambda n, s, dt=F32: es35.enter_context(nc.sbuf_tensor(k.fresh(n), list(s), dt))
    I32 = mybir.dt.int32
    iug_i = sb("iugi", [128, NTG, 8], I32)
    idn_i = sb("idni", [128, NTG, NFC], I32)
    iug_f = sb35("iugf", [128, NTG, 8])
    idn_f = sb35("idnf", [128, NTG, NFC])
    pipe = Pipe(k)

    def f_ebc(te, banks):
        return te.matmul(banks[0][:, 0:NTG], g["ones1"], tgexp_f[0:1, 0:NTG],
                         start=True, stop=True)

    def f_ebc_ve(ve, banks):
        iob = g["iotacol"].broadcast_to((128, NTG))
        for dc in range(8):
            ve.scalar_tensor_tensor(iug_f[:, :, dc], banks[0][:, 0:NTG], 1024.0, iob,
                                    ALU.mult, ALU.add)
            ve.tensor_scalar(iug_f[:, :, dc], iug_f[:, :, dc], float(dc * 128), None, ALU.add)
        for ec in range(NFC):
            ve.scalar_tensor_tensor(idn_f[:, :, ec], banks[0][:, 0:NTG], float(DE), iob,
                                    ALU.mult, ALU.add)
            ve.tensor_scalar(idn_f[:, :, ec], idn_f[:, :, ec], float(ec * 128), None, ALU.add)
        ve.tensor_copy(iug_i[:], iug_f[:])
        return ve.tensor_copy(idn_i[:], idn_f[:])
    pipe.job(pe=f_ebc, nb=1, steps=[("vector", f_ebc_ve)], after=[scat_done])
    with nc.Block() as blk:
        pipe.run(blk)
    es35.close()

    es4 = contextlib.ExitStack()
    sb4 = lambda n, s, dt=F32: es4.enter_context(nc.sbuf_tensor(k.fresh(n), list(s), dt))
    xg = sb4("xg", [128, 2, 2, 1024])
    xgT = sb4("xgT", [128, 8, 256])
    hid = sb4("hidS", [128, NFC, 256])
    eo_sb = sb4("eosb", [128, 2, 2, 1024])
    gsb = sb4("gsb", [128, 2, 8, 256])
    if layer == 0:
        wug = [dr["w_gate0"].rearrange("a b c -> (a b) c"), dr["w_up0"].rearrange("a b c -> (a b) c")]
        wdn = dr["w_down0"].rearrange("a b c -> (a b) c")
    else:
        wug = [dr["f1w1"].rearrange("a b c -> (a b) c"), dr["f1w2"].rearrange("a b c -> (a b) c")]
        wdn = dr["f1w3"].rearrange("a b c -> (a b) c")
    NPASS = (NFC + 7) // 8

    class GRing:
        def __init__(self):
            self.buf = sb4(k.fresh("grb"), [128, 8, 1024])
            self.ns = 8
            self.ctrs = [k.ctr(k.fresh("grs")) for _ in range(8)]
            self.loads = []
            self.freed = {}

        def add(self, src_ap, idx_ap, eoff, cols):
            self.loads.append((src_ap, idx_ap, eoff, cols))
            return len(self.loads) - 1

        def tile(self, i):
            return self.buf[:, i % self.ns, 0:self.loads[i][3]]

        def wait(self, te, i):
            s = i % self.ns
            te.wait_ge(self.ctrs[s].sem, 16 * (i // self.ns + 1))

        def emit_loader(self, block):
            loads, ns, ctrs = self.loads, self.ns, self.ctrs
            buf, freed = self.buf, self.freed

            @block.gpsimd
            def _(gpe):
                for i, (src_ap, idx_ap, eoff, cols) in enumerate(loads):
                    if i >= ns:
                        c, v = freed[i - ns]
                        gpe.wait_ge(c.sem, v)
                    gpe.indirect_dma_start(
                        out=buf[:, i % ns, 0:cols], out_offset=None, in_=src_ap,
                        in_offset=bass.IndirectOffsetOnAxis(ap=idx_ap, axis=0),
                        element_offset=eoff).then_inc(ctrs[i % ns].sem, 16)
            for i in range(len(loads)):
                ctrs[i % ns].n += 16

    ringd = GRing()
    xlc = [k.ctr(k.fresh("xlc")) for _ in range(2)]
    stoc = k.ctr(k.fresh("stoc"))
    pipe = Pipe(k)
    pipe.rings = [ringd]
    j_tr_by_tg = {}
    j_st_prev = None
    j_dn_prev = None
    j_up_prev = None
    for tg in range(NTG):
        par = tg % 2

        def f_xl(sync, banks, tg=tg, par=par):
            return sync.dma_start(out=xg[:, par, 0, :], in_=xg2_dram[tg * 256: tg * 256 + 128, :])

        def f_xl2(sync, banks, tg=tg, par=par):
            return sync.dma_start(out=xg[:, par, 1, :], in_=xg2_dram[tg * 256 + 128: tg * 256 + 256, :])
        aft_xl = [scat_done]
        if tg >= 2:
            aft_xl.append((j_tr_by_tg[tg - 2][-1], "pe"))
        j_xl = pipe.job(steps=[("sync", f_xl, 16, xlc[par]), ("sync", f_xl2, 16, xlc[par])],
                        after=aft_xl)

        jtr = []
        for jb in range(2):
            for dc in range(8):
                def f_pe(te, banks, par=par, jb=jb, dc=dc):
                    return te.transpose(banks[0][:, 0:128], xg[:, par, jb, dc * 128:(dc + 1) * 128],
                                        ident[:])

                def f_ev(ve, banks, jb=jb, dc=dc):
                    return ve.tensor_copy(xgT[:, dc, jb * 128:(jb + 1) * 128], banks[0][:, 0:128])
                jtr.append(pipe.job(pe=f_pe, nb=1, steps=[("vector", f_ev)], after=[(j_xl, 1)]))
        j_tr_by_tg[tg] = jtr

        for ps in range(NPASS):
            fcs = list(range(ps * 8, min(ps * 8 + 8, NFC)))
            nfc = len(fcs)
            ppar = ps % 2
            jg_cur = None
            for wi, wh in ((0, wug[0]), (1, wug[1])):
                idxs = [ringd.add(wh, iug_i[:, tg, dc:dc + 1], ps * 1024, nfc * 128)
                        for dc in range(8)]

                def mk_upe(d0, d1, idxs=idxs, nfc=nfc):
                    def f_pe(te, banks):
                        last = None
                        for dc in range(d0, d1):
                            ringd.wait(te, idxs[dc])
                            tl = ringd.tile(idxs[dc])
                            for i in range(nfc):
                                last = te.matmul(banks[i][:, 0:256], tl[:, i * 128:(i + 1) * 128],
                                                 xgT[:, dc, :], start=(dc == 0), stop=(dc == 7))
                        return last
                    return f_pe

                if wi == 0:
                    def f_ev(sc, banks, nfc=nfc, ppar=ppar):
                        last = None
                        for i in range(nfc):
                            last = sc.activation(gsb[:, ppar, i, :], banks[i][:, 0:256], AF.Silu)
                        return last
                    aft = [(jtr[-1], 0)]
                    if j_dn_prev is not None and ps == 0:
                        aft.append((j_dn_prev, 0))
                    if j_up_prev is not None:
                        aft.append((j_up_prev, 0))
                    ja = pipe.job(pe=mk_upe(0, 4), nb=nfc, after=aft, wtiles=(ringd, idxs[:4]))
                    jg_cur = pipe.job(pe=mk_upe(4, 8), banks_from=ja,
                                      steps=[("scalar", f_ev)], wtiles=(ringd, idxs[4:]))
                else:
                    def f_ev(ve, banks, fcs=fcs, nfc=nfc, ppar=ppar):
                        last = None
                        for i in range(nfc):
                            last = ve.tensor_tensor(hid[:, fcs[i], :], banks[i][:, 0:256],
                                                    gsb[:, ppar, i, :], ALU.mult)
                        return last
                    ja = pipe.job(pe=mk_upe(0, 4), nb=nfc, after=[(jg_cur, 0)],
                                  wtiles=(ringd, idxs[:4]))
                    j_up_prev = pipe.job(pe=mk_upe(4, 8), banks_from=ja,
                                         steps=[("vector", f_ev)], wtiles=(ringd, idxs[4:]))

        idxs = [ringd.add(wdn, idn_i[:, tg, ec:ec + 1], 0, 1024) for ec in range(NFC)]

        def mk_pe(e0, e1, idxs=idxs):
            def f_pe(te, banks):
                last = None
                for ec in range(e0, e1):
                    ringd.wait(te, idxs[ec])
                    tl = ringd.tile(idxs[ec])
                    for jb in range(2):
                        for nd in range(2):
                            last = te.matmul(banks[jb * 2 + nd][:, 0:512],
                                             hid[:, ec, jb * 128:(jb + 1) * 128],
                                             tl[:, nd * 512:(nd + 1) * 512],
                                             start=(ec == 0), stop=(ec == NFC - 1))
                return last
            return f_pe

        def f_ev(ve, banks, par=par):
            last = None
            for jb in range(2):
                for nd in range(2):
                    last = ve.tensor_copy(eo_sb[:, par, jb, nd * 512:(nd + 1) * 512],
                                          banks[jb * 2 + nd][:, 0:512])
            return last
        aft = [(j_up_prev, 0)]
        if j_st_prev is not None:
            aft.append((j_st_prev, 0))
        bounds = list(range(0, NFC, 4)) + [NFC]
        jprev_dn = None
        for bi in range(len(bounds) - 1):
            e0, e1 = bounds[bi], bounds[bi + 1]
            lastsub = (bi == len(bounds) - 2)
            j_dn_prev = pipe.job(
                pe=mk_pe(e0, e1), nb=(4 if bi == 0 else 0),
                banks_from=(None if bi == 0 else jprev_dn),
                steps=([("vector", f_ev)] if lastsub else []),
                after=(aft if bi == 0 else []),
                wtiles=(ringd, idxs[e0:e1]))
            jprev_dn = j_dn_prev

        def f_st(se, banks, tg=tg, par=par):
            return se.dma_start(out=eo_dram[tg * 256: tg * 256 + 128, :], in_=eo_sb[:, par, 0, :])

        def f_st2(se, banks, tg=tg, par=par):
            return se.dma_start(out=eo_dram[tg * 256 + 128: tg * 256 + 256, :],
                                  in_=eo_sb[:, par, 1, :])
        j_st_prev = pipe.job(steps=[("scalar", f_st, 16, stoc), ("scalar", f_st2, 16, stoc)],
                             after=[(j_dn_prev, 0)])

    with nc.Block() as blk:
        pipe.run(blk)
    es4.close()
    eo_done = ("ctr", stoc, stoc.n)

    es5 = contextlib.ExitStack()
    sb5 = lambda n, s, dt=F32: es5.enter_context(nc.sbuf_tensor(k.fresh(n), list(s), dt))
    g1 = sb5("g1", [128, D])
    g2 = sb5("g2", [128, D])
    c1 = sb5("c1", [128, D])
    psts = [sb5("pstd", [128, 8]) for _ in range(4)]
    if layer == 0:
        xT = sb5("xTD", [128, 8, 512])
        hidD = sb5("hidD", [128, 16, 512])
        gsilD = sb5("gsilD", [128, 2, 4, 512])
        yacc = sb5("yaccD", [128, 4, D])
        ring = WRing(k, k.fresh("wrDSH"), 8, sb=sb5)

    for tile in range(4 if layer == 0 else 1):
        pipe = Pipe(k)
        tbs = range(tile * 4, tile * 4 + 4) if layer == 0 else range(16)
        if layer == 0:
            pipe.rings = [ring]
            ring.begin()
            ring.first_wait = (eo_done[1], eo_done[2])
            t0 = tile * 4
            jtr_last = None
            for tb in range(4):
                for dc in range(8):
                    def f_pe(te, banks, tb=tb, dc=dc):
                        return te.transpose(banks[0][:, 0:128],
                                            x_res[:, t0 + tb, dc * 128:(dc + 1) * 128], ident[:])

                    def f_ev(ve, banks, tb=tb, dc=dc):
                        return ve.tensor_copy(xT[:, dc, tb * 128:(tb + 1) * 128], banks[0][:, 0:128])
                    jtr_last = pipe.job(pe=f_pe, nb=1, steps=[("vector", f_ev)],
                                        after=([eo_done] if tb == 0 and dc == 0 else []))
            jup = []
            j_gt = None
            for grp in range(4):
                fcs = list(range(grp * 4, grp * 4 + 4))
                gpar = grp % 2
                for which, wnm in (("g", "ws_gate0"), ("u", "ws_up0")):
                    idxs = [ring.add(dr[wnm][dc * 128:(dc + 1) * 128,
                                             fcs[0] * 128: fcs[0] * 128 + 512], 128, 512)
                            for dc in range(8)]

                    def f_pe(te, banks, idxs=idxs):
                        last = None
                        for dc in range(8):
                            ring.wait(te, idxs[dc])
                            tl = ring.tile(idxs[dc])
                            for i in range(4):
                                last = te.matmul(banks[i][:, 0:512], tl[:, i * 128:(i + 1) * 128],
                                                 xT[:, dc, :], start=(dc == 0), stop=(dc == 7))
                        return last

                    if which == "g":
                        def f_ev(sc, banks, gpar=gpar):
                            last = None
                            for i in range(4):
                                last = sc.activation(gsilD[:, gpar, i, :], banks[i][:, 0:512], AF.Silu)
                            return last
                        aft = [(jtr_last, 0)]
                        if len(jup) >= 2:
                            aft.append((jup[-2], 0))
                        j_gt = pipe.job(pe=f_pe, nb=4, steps=[("scalar", f_ev)], after=aft,
                                        wtiles=(ring, idxs))
                    else:
                        def f_ev(ve, banks, fcs=fcs, gpar=gpar):
                            last = None
                            for i in range(4):
                                last = ve.tensor_tensor(hidD[:, fcs[i], :], banks[i][:, 0:512],
                                                        gsilD[:, gpar, i, :], ALU.mult)
                            return last
                        jup.append(pipe.job(pe=f_pe, nb=4, steps=[("vector", f_ev)],
                                            after=[(j_gt, 0)], wtiles=(ring, idxs)))
            j_dn_last = None
            for nd in range(2):
                idxs = [ring.add(dr["ws_down0"][ec * 128:(ec + 1) * 128, nd * 512:(nd + 1) * 512],
                                 128, 512) for ec in range(16)]

                def mk_pe(e0, e1, idxs=idxs):
                    def f_pe(te, banks):
                        last = None
                        for ec in range(e0, e1):
                            ring.wait(te, idxs[ec])
                            tl = ring.tile(idxs[ec])
                            for tb in range(4):
                                last = te.matmul(banks[tb][:, 0:512],
                                                 hidD[:, ec, tb * 128:(tb + 1) * 128],
                                                 tl, start=(ec == 0), stop=(ec == 15))
                        return last
                    return f_pe

                def f_ev(ve, banks, nd=nd):
                    last = None
                    for tb in range(4):
                        last = ve.tensor_copy(yacc[:, tb, nd * 512:(nd + 1) * 512],
                                              banks[tb][:, 0:512])
                    return last
                jpd = None
                for bi, (e0, e1) in enumerate([(0, 8), (8, 16)]):
                    j_dn_last = pipe.job(pe=mk_pe(e0, e1), nb=(4 if bi == 0 else 0),
                                         banks_from=(None if bi == 0 else jpd),
                                         steps=([("vector", f_ev)] if bi == 1 else []),
                                         after=([(jup[-1], 0)] if bi == 0 else []),
                                         wtiles=(ring, idxs[e0:e1]))
                    jpd = j_dn_last
            aft_base = [(j_dn_last, 0)]
        else:
            aft_base = []

        jcomb_prev = None
        jln_prev = None
        for ti, tb in enumerate(tbs):
            xt = x_res[:, tb, :]

            def f_gb(gpe, banks, tb=tb):
                return gpe.indirect_dma_start(
                    out=g1[:], out_offset=None, in_=eo_dram,
                    in_offset=bass.IndirectOffsetOnAxis(ap=dsel_i[:, tb, 0:1], axis=0))

            def f_gb2(gpe, banks, tb=tb):
                return gpe.indirect_dma_start(
                    out=g2[:], out_offset=None, in_=eo_dram,
                    in_offset=bass.IndirectOffsetOnAxis(ap=dsel_i[:, tb, 1:2], axis=0))
            aft_gb = list(aft_base) + [eo_done]
            if jcomb_prev is not None:
                aft_gb.append((jcomb_prev, 0))
            j_gb = pipe.job(steps=[("gpsimd", f_gb, 16), ("gpsimd", f_gb2, 16)], after=aft_gb)

            def f_cmb(ve, banks, tb=tb, ti=ti, xt=xt):
                ve.tensor_scalar(c1[:], g1[:], wsel[:, tb, 0:1], None, ALU.mult)
                ve.scalar_tensor_tensor(c1[:], g2[:], wsel[:, tb, 1:2], c1[:], ALU.mult, ALU.add)
                if layer == 0:
                    ve.tensor_tensor(c1[:], c1[:], yacc[:, ti, :], ALU.add)
                    return ve.tensor_tensor(xsum[:], xt, c1[:], ALU.add)
                return ve.scalar_tensor_tensor(xsum[:], xt, 2.0, c1[:], ALU.mult, ALU.add)
            aft_c = [(j_gb, 1)]
            if jln_prev is not None:
                aft_c.extend([(jln_prev, 3), (jln_prev, 2)])
            jcomb_prev = pipe.job(steps=[("vector", f_cmb)], after=aft_c)
            jln_prev = emit_ln(pipe, xsum[:], xt, psts[ti % 4], nsq[:], g["epsc"][:, 1:2],
                               after=[(jcomb_prev, 0)])

        with nc.Block() as blk:
            pipe.run(blk)
    es5.close()
    es2.close()
